# version 01
# baseline (speedup 1.0000x reference)
import numpy as np
import concourse.bass as bass
import concourse.mybir as mybir
from concourse.bass_utils import run_bass_kernel_spmd
from concourse.bass_core_rust import IndirectOffsetOnAxis

F32 = mybir.dt.float32
BF16 = mybir.dt.bfloat16
FP8 = mybir.dt.float8e4
U32 = mybir.dt.uint32
AF = mybir.ActivationFunctionType
ALU = mybir.AluOpType
AX = mybir.AxisListType

BATCH, T, H = 4, 4096, 2048
NCORE = 8
TOK = 2048
HALO = 512
TE = TOK + HALO
NB = TE // 128
NQ = TOK // 128
NSLAB = NB // 4
D, NHEAD, NKVH = 128, 16, 4
NHC = H // 128
II = 5632
NJ = II // 128
CC = 256
NCH = TOK // CC
SCALE = float(1.0 / np.sqrt(128.0))
NEGINF = -1.0e30
LN_EPS = 1e-5
TAP_CC = 0
TAP_J = 9

ENGINES = ("sync", "gpsimd", "tensor", "vector", "scalar")


class Buf:
    __slots__ = ("ready", "reads")

    def __init__(self):
        self.ready = None
        self.reads = []


class Sched:

    def __init__(self):
        self.prog = {e: [] for e in ENGINES}
        self.cnt = {}
        self.hwm = {e: {} for e in ENGINES}

    def step(self, eng, fn, reads=(), writes=(), sem=None, dma=False,
             extra_waits=(), narrow=False):
        if sem is None:
            sem = ("q_" + eng) if dma else ("c_" + eng)
        amt = 16 if dma else 1
        waits = {}

        def addw(dep):
            peng, psem, pcnt, upg = dep
            if peng != eng:
                if upg:
                    pcnt = self.cnt[psem]
                waits[psem] = max(waits.get(psem, 0), pcnt)

        for b in reads:
            if b.ready is not None:
                addw(b.ready)
        for b in writes:
            if b.ready is not None:
                addw(b.ready)
            for dep in b.reads:
                addw(dep)
        for psem, pcnt in extra_waits:
            waits[psem] = max(waits.get(psem, 0), pcnt)
        hwm = self.hwm[eng]
        wl = []
        for n in sorted(waits):
            v = waits[n]
            if v > hwm.get(n, 0):
                wl.append((n, v))
                hwm[n] = v
        self.cnt[sem] = self.cnt.get(sem, 0) + amt
        upg = dma and eng == "gpsimd"
        me = ((None if dma else eng), sem, self.cnt[sem], upg)
        for b in writes:
            b.ready = me
            b.reads = []
        for b in reads:
            b.reads.append(me)
        if narrow:
            inner = fn

            def fn(e, _inner=inner):
                _inner(e)
                return e.tensor_copy(self.junk[1][:], self.junk[0][:])
        self.prog[eng].append((wl, fn, sem, amt))
        return me

    def barrier(self):
        snap = [(n, c) for n, c in self.cnt.items()]
        for eng in ENGINES:
            self.step(eng, lambda e: e.nop(), extra_waits=snap)

    def finalize(self, nc, sems):
        with nc.Block() as blk:
            for eng_name in ENGINES:
                def body(e, _eng=eng_name):
                    for wl, fn, sem, amt in self.prog[_eng]:
                        for n, v in wl:
                            e.wait_ge(sems[n], v)
                        inst = fn(e)
                        if inst is None:
                            inst = e.nop()
                        inst.then_inc(sems[sem], amt)
                getattr(blk, eng_name)(body)



def _rope_tables():
    inv = 1.0 / (10000.0 ** (np.arange(0, D, 2, dtype=np.float64) / D))
    tabs = []
    for hh in (0, 1):
        pos = np.arange(hh * TOK - HALO, hh * TOK + TOK, dtype=np.float64)
        fr = pos[:, None] * inv[None, :]
        cos = np.concatenate([np.cos(fr), np.cos(fr)], axis=1)
        sin = np.concatenate([np.sin(fr), np.sin(fr)], axis=1)
        tabs.append((np.ascontiguousarray(cos.T.astype(np.float32)),
                     np.ascontiguousarray(sin.T.astype(np.float32))))
    return tabs


def _masks():
    out = []
    for hh in (0, 1):
        m = np.full((NQ, 5, 128, 128), NEGINF, dtype=np.float32)
        for ti in range(NQ):
            q = hh * TOK + ti * 128 + np.arange(128)
            for kb in range(5):
                k = hh * TOK - HALO + (ti + kb) * 128 + np.arange(128)
                dif = q[None, :] - k[:, None]
                ok = (dif >= 0) & (dif < HALO) & (k[:, None] >= 0)
                m[ti, kb][ok] = 0.0
        out.append(m)
    return out


def _rot_lhsT():
    M = np.zeros((128, 128), dtype=np.float32)
    for d in range(64):
        M[d, d + 64] = -1.0
        M[d + 64, d] = 1.0
    return np.ascontiguousarray(M.T)


def stage_inputs(L0_attn_x, L0_attn_wq, L0_attn_wk, L0_attn_wv, L0_attn_wo,
                 L0_ffn_w_gate, L0_ffn_w_up, L0_ffn_w_down,
                 L0_ffn_up_reorder_index, L0_ffn_down_reorder_index):
    x = np.asarray(L0_attn_x, dtype=np.float32)
    wq = np.ascontiguousarray(np.asarray(L0_attn_wq, np.float32)
                              .reshape(H, NHEAD, 128).transpose(1, 0, 2))
    wk = np.ascontiguousarray(np.asarray(L0_attn_wk, np.float32)
                              .reshape(H, NKVH, 128).transpose(1, 0, 2))
    wv = np.ascontiguousarray(np.asarray(L0_attn_wv, np.float32)
                              .reshape(H, NKVH, 128).transpose(1, 0, 2))
    wo = np.ascontiguousarray(np.asarray(L0_attn_wo, np.float32))
    wg = np.ascontiguousarray(np.asarray(L0_ffn_w_gate, np.float32))
    wu = np.ascontiguousarray(np.asarray(L0_ffn_w_up, np.float32))
    wd = np.ascontiguousarray(np.asarray(L0_ffn_w_down, np.float32))
    didx = np.ascontiguousarray(
        np.asarray(L0_ffn_down_reorder_index).astype(np.uint32))
    rope = _rope_tables()
    masks = _masks()
    rmat = _rot_lhsT()
    ident = np.ascontiguousarray(np.eye(128, dtype=np.float32))
    ones = np.ones((128, 128), dtype=np.float32)
    consts = np.zeros((128, 2), dtype=np.float32)
    consts[:, 0] = LN_EPS
    consts[:, 1] = 1.0 / H
    in_maps = []
    for c in range(NCORE):
        b, hh = c // 2, c % 2
        xs = np.zeros((TE, H), dtype=np.float32)
        lo = hh * TOK - HALO
        src_lo = max(lo, 0)
        xs[src_lo - lo:] = x[b, src_lo:hh * TOK + TOK]
        cosT, sinT = rope[hh]
        in_maps.append({
            "xs": np.ascontiguousarray(xs),
            "wq": wq, "wk": wk, "wv": wv, "wo": wo,
            "wg": wg, "wu": wu, "wd": wd, "didx": didx,
            "rcosT": cosT, "rsinT": sinT, "maskT": masks[hh],
            "rmat": rmat, "ident": ident, "ones": ones, "consts": consts,
        })
    return in_maps



def build_program(phases=("A", "B", "C"), taps=()):
    nc = bass.Bass("TRN2", target_bir_lowering=False, debug=False)
    S = Sched()

    d_xs = nc.dram_tensor("xs", [TE, H], F32, kind="ExternalInput").ap()
    d_wq = nc.dram_tensor("wq", [NHEAD, H, 128], F32, kind="ExternalInput").ap()
    d_wk = nc.dram_tensor("wk", [NKVH, H, 128], F32, kind="ExternalInput").ap()
    d_wv = nc.dram_tensor("wv", [NKVH, H, 128], F32, kind="ExternalInput").ap()
    d_wo = nc.dram_tensor("wo", [H, H], F32, kind="ExternalInput").ap()
    d_wg = nc.dram_tensor("wg", [II, H], F32, kind="ExternalInput").ap()
    d_wu = nc.dram_tensor("wu", [II, H], F32, kind="ExternalInput").ap()
    d_wd = nc.dram_tensor("wd", [H, II], F32, kind="ExternalInput").ap()
    d_didx = nc.dram_tensor("didx", [II], U32, kind="ExternalInput").ap()
    d_cos = nc.dram_tensor("rcosT", [128, TE], F32, kind="ExternalInput").ap()
    d_sin = nc.dram_tensor("rsinT", [128, TE], F32, kind="ExternalInput").ap()
    d_mask = nc.dram_tensor("maskT", [NQ, 5, 128, 128], F32,
                            kind="ExternalInput").ap()
    d_rmat = nc.dram_tensor("rmat", [128, 128], F32, kind="ExternalInput").ap()
    d_ident = nc.dram_tensor("ident", [128, 128], F32, kind="ExternalInput").ap()
    d_ones = nc.dram_tensor("ones", [128, 128], F32, kind="ExternalInput").ap()
    d_consts = nc.dram_tensor("consts", [128, 2], F32, kind="ExternalInput").ap()
    d_out = nc.dram_tensor("out", [TOK, H], F32, kind="ExternalOutput").ap()
    d_g8 = nc.dram_tensor("scr_g8", [NJ, 128, H], FP8, kind="Internal").ap()
    d_u8 = nc.dram_tensor("scr_u8", [NJ, 128, H], FP8, kind="Internal").ap()
    d_d8 = nc.dram_tensor("scr_d8", [NHC, 128, II], FP8, kind="Internal").ap()
    taps = set(taps)

    def tap(name, sb_ap, buf, eng="gpsimd"):
        if name not in taps:
            return
        shape = list(sb_ap.shape)
        d = nc.dram_tensor("tap_" + name, shape, sb_ap.dtype,
                           kind="ExternalOutput").ap()
        bl = buf if isinstance(buf, (list, tuple)) else [buf]
        dma(eng, d, sb_ap, reads=bl, writes=[gb("tapw")])

    sb_base = (nc._sbuf_addr_for_side("left") + 31) & ~31
    sb_limit = sb_base + nc.sbuf_bytes_remaining - 64
    off = [sb_base]
    sb = {}

    def alloc(name, shape, dtype, at=None):
        per = int(np.prod(shape[1:])) * mybir.dt.size(dtype)
        if at is None:
            at = off[0]
            off[0] = (at + per + 31) & ~31
        h = nc.alloc_sbuf_tensor_at(name, shape, dtype, offset=at)
        sb[name] = h
        return h

    consts = alloc("c_consts", [128, 2], F32)
    ident = alloc("c_ident", [128, 128], F32)
    ident8 = alloc("c_ident8", [128, 128], FP8)
    rmat = alloc("c_rmat", [128, 128], F32)
    ones = alloc("c_ones", [128, 128], F32)
    junk0 = alloc("c_junk0", [128, 256], F32)
    junk1 = alloc("c_junk1", [128, 256], F32)
    S.junk = (junk0, junk1)
    common_end = off[0]

    off[0] = common_end
    cosb = [alloc(f"a_cos{i}", [128, 512], F32) for i in range(2)]
    sinb = [alloc(f"a_sin{i}", [128, 512], F32) for i in range(2)]
    xring = [alloc(f"a_x{i}", [128, H], F32) for i in range(3)]
    pool2k = [alloc(f"a_p2k{i}", [128, H], F32) for i in range(3)]
    xnT = alloc("a_xnT", [128, NHC * 512], F32)
    wbuf = [alloc(f"a_wb{i}", [128, 8 * 128], F32) for i in range(2)]
    qT = alloc("a_qT", [128, NHEAD * 512], F32)
    kT = alloc("a_kT", [128, 2 * NKVH * 512], F32)
    vC = alloc("a_vC", [128, 2 * NKVH * 512], F32)
    scr = [alloc(f"a_scr{i}", [128, 512], F32) for i in range(5)]
    maskb = [alloc(f"a_mb{i}", [128, 640], F32) for i in range(2)]
    prE = alloc("a_prE", [128, 640], F32)
    rz = alloc("a_rz", [128, 128], F32)
    concat = alloc("a_concat", [128, 2 * NHC * 128], F32)
    wob = [alloc(f"a_wo{i}", [128, 1024], F32) for i in range(2)]
    a_end = off[0]
    assert a_end <= sb_limit, f"phase A SBUF over: {a_end - sb_base}"

    off[0] = common_end
    b_w = [alloc(f"b_w{i}", [128, II], F32) for i in range(2)]
    b_w8 = [alloc(f"b_w8{i}", [128, II], FP8) for i in range(2)]
    b_stage = [alloc(f"b_st{i}", [128, II], FP8) for i in range(2)]
    b_idx = alloc("b_idx", [128, NJ], U32)
    b_wsg = alloc("b_wsg", [128, NJ], F32)
    b_wsu = alloc("b_wsu", [128, NJ], F32)
    b_ws2 = alloc("b_ws2", [128, NHC], F32)
    b_m = alloc("b_m", [128, 4], F32)
    b_end = off[0]
    assert b_end <= sb_limit, f"phase B SBUF over: {b_end - sb_base}"

    b_keep_end = b_end
    off[0] = b_keep_end
    c_x = [alloc(f"c_x{i}", [128, H], F32) for i in range(2)]
    c_xn = alloc("c_xn", [128, H], F32)
    c_qx8 = alloc("c_qx8", [128, H], FP8)
    c_qx8T = alloc("c_qx8T", [128, NHC * CC], FP8)
    c_fx = alloc("c_fx", [128, 2], F32)
    c_fxb = alloc("c_fxb", [128, CC], F32)
    c_wg8 = [alloc(f"c_wg8{i}", [128, H], FP8) for i in range(2)]
    c_wu8 = [alloc(f"c_wu8{i}", [128, H], FP8) for i in range(2)]
    c_t = alloc("c_t", [128, NJ * CC], F32)
    c_rmax = alloc("c_rmax", [128, CC], F32)
    c_rmin = alloc("c_rmin", [128, CC], F32)
    c_qt8 = alloc("c_qt8", [128, NJ * CC], FP8)
    c_wd8 = [alloc(f"c_wd8{i}", [128, II], FP8) for i in range(2)]
    c_qsb = alloc("c_qsb", [128, CC], F32)
    c_ftb = alloc("c_ftb", [128, CC], F32)
    c_y2 = [alloc(f"c_y2{i}", [128, 512], F32) for i in range(2)]
    c_ost = [alloc(f"c_ost{i}", [128, H], F32) for i in range(2)]
    c_mrow = alloc("c_mrow", [128, 8], F32)
    c_stats = alloc("c_stats", [128, 8], F32)
    c_gsc = alloc("c_gsc", [128, CC], F32)
    c_usc = alloc("c_usc", [128, CC], F32)
    c_gsi = alloc("c_gsi", [128, CC], F32)
    c_row = alloc("c_row", [128, 512], F32)
    c_end = off[0]
    assert c_end <= sb_limit, f"phase C SBUF over: {c_end - sb_base}"

    pb = [nc.alloc_psum_tensor(f"pb{i}", [128, 512], F32) for i in range(6)]
    score_ps = nc.alloc_psum_tensor("score_ps", [128, 640], F32)

    bufs = {}

    def gb(name):
        if name not in bufs:
            bufs[name] = Buf()
        return bufs[name]

    pbB = [gb(f"pb{i}") for i in range(6)]
    scoreB = gb("score")

    def dma(eng, out_ap, in_ap, reads=(), writes=(), **kw):
        return S.step(eng, lambda e: e.dma_start(out=out_ap, in_=in_ap, **kw),
                      reads=reads, writes=writes, dma=True)

    cB = gb("consts")
    dma("sync", consts[:], d_consts, writes=[cB])
    dma("sync", ident[:], d_ident, writes=[gb("ident")])
    dma("sync", rmat[:], d_rmat, writes=[gb("rmat")])
    dma("sync", ones[:], d_ones, writes=[gb("ones")])
    S.step("vector", lambda e: e.tensor_copy(ident8[:], ident[:]),
           reads=[gb("ident")], writes=[gb("ident8")])

    EPS = consts[:, 0:1]
    INVH = consts[:, 1:2]

    def emit_ln_g(xsb, xbuf, xnsb, xnbuf, lnscr, lnscrB, stats, statsB):
        S.step("scalar",
               lambda e: e.activation(lnscr[:], xsb[:], AF.Copy,
                                      accum_out=stats[:, 0:1]),
               reads=[xbuf], writes=[lnscrB, statsB])
        S.step("vector",
               lambda e: e.tensor_scalar(stats[:, 1:2], stats[:, 0:1],
                                         -1.0 / H, None, ALU.mult),
               reads=[statsB], writes=[statsB])
        S.step("scalar",
               lambda e: e.activation(xnsb[:], xsb[:], AF.Identity,
                                      bias=stats[:, 1:2]),
               reads=[xbuf, statsB], writes=[xnbuf])
        S.step("vector",
               lambda e: e.scalar_tensor_tensor(
                   lnscr[:], xnsb[:], 1.0, xnsb[:], ALU.bypass, ALU.mult,
                   accum_out=stats[:, 2:3]),
               reads=[xnbuf], writes=[lnscrB, statsB])
        S.step("scalar",
               lambda e: e.activation(stats[:, 3:4], stats[:, 2:3], AF.Sqrt,
                                      bias=EPS, scale=INVH),
               reads=[statsB, cB], writes=[statsB])
        S.step("vector",
               lambda e: e.reciprocal(stats[:, 4:5], stats[:, 3:4]),
               reads=[statsB], writes=[statsB])
        S.step("scalar",
               lambda e: e.activation(xnsb[:], xnsb[:], AF.Copy,
                                      scale=stats[:, 4:5]),
               reads=[statsB, xnbuf], writes=[xnbuf])

    if "A" in phases:
        lnscr = pool2k[2]
        lnscrB = gb("lnscr")
        stats = scr[4]
        statsB = gb("stats")

        def emit_ln(xsb, xbuf, xnsb, xnbuf):
            emit_ln_g(xsb, xbuf, xnsb, xnbuf, lnscr, lnscrB, stats, statsB)

        xnTB = gb("xnT")
        qTB = gb("qT")
        kTB = [gb(f"kT{i}") for i in range(8)]
        vCB = [gb(f"vC{i}") for i in range(8)]
        cosB = [gb(f"cos{i}") for i in range(2)]
        maskB = [gb(f"mask{i}") for i in range(2)]
        concatB = gb("concat")
        prEB = gb("prE")
        rzB = gb("rz")
        woB = [gb(f"wo{i}") for i in range(2)]
        wbB = [gb(f"wb{i}") for i in range(2)]
        xrB = [gb(f"xr{i}") for i in range(3)]
        p2kB = [gb(f"p2k{i}") for i in range(3)]

        def proj_epilogue_qk(oc, bank, bank_buf, dst_sb, dst_col, dst_buf,
                             cos_sb, sin_sb, cs_buf):
            sq, sqB = scr[0], gb("scr0")
            nrm, nrmB = scr[1], gb("scr1")
            qn, qnB = scr[2], gb("scr2")
            t1, t1B = scr[3], gb("scr3")
            S.step("scalar", lambda e: e.activation(sq[:], bank[:], AF.Square),
                   reads=[bank_buf], writes=[sqB])
            S.step("tensor",
                   lambda e: e.matmul(pb[2][:], ones[:], sq[:],
                                      start=True, stop=True),
                   reads=[sqB, gb("ones")], writes=[pbB[2]])
            S.step("scalar", lambda e: e.activation(nrm[:], pb[2][:], AF.Sqrt),
                   reads=[pbB[2]], writes=[nrmB])
            S.step("vector",
                   lambda e: e.tensor_scalar(nrm[:], nrm[:], 1e-12, None,
                                             ALU.max),
                   reads=[nrmB], writes=[nrmB])
            S.step("vector", lambda e: e.reciprocal(nrm[:], nrm[:]),
                   reads=[nrmB], writes=[nrmB])
            S.step("vector", lambda e: e.tensor_mul(qn[:], bank[:], nrm[:]),
                   reads=[bank_buf, nrmB], writes=[qnB])
            S.step("tensor",
                   lambda e: e.matmul(pb[3][:], rmat[:], qn[:],
                                      start=True, stop=True),
                   reads=[qnB, gb("rmat")], writes=[pbB[3]])
            S.step("gpsimd", lambda e: e.tensor_mul(t1[:], qn[:], cos_sb[:]),
                   reads=[qnB, cs_buf], writes=[t1B])
            S.step("vector",
                   lambda e: e.scalar_tensor_tensor(
                       qn[:], pb[3][:], 1.0, sin_sb[:], ALU.bypass, ALU.mult),
                   reads=[pbB[3], cs_buf], writes=[qnB])
            S.step("vector",
                   lambda e: e.tensor_add(dst_sb[:, dst_col:dst_col + 512],
                                          qn[:], t1[:]),
                   reads=[qnB, t1B], writes=[dst_buf])

        for s in range(NSLAB):
            par = s % 2
            dma("sync", cosb[par][:], d_cos[:, s * 512:(s + 1) * 512],
                writes=[cosB[par]])
            dma("sync", sinb[par][:], d_sin[:, s * 512:(s + 1) * 512],
                writes=[gb(f"sin{par}")])
            for bpos in range(4):
                eb = 4 * s + bpos
                xs_sb, xs_buf = xring[eb % 3], xrB[eb % 3]
                dma("sync", xs_sb[:], d_xs[eb * 128:(eb + 1) * 128, :],
                    writes=[xs_buf])
                xn_sb, xn_buf = pool2k[eb % 2], p2kB[eb % 2]
                emit_ln(xs_sb, xs_buf, xn_sb, xn_buf)
                tap(f"xn_e{eb}", xn_sb[:], xn_buf)
                for g in range(4):
                    bk, bkB = pb[4], pbB[4]
                    def tr_group(e, g=g, xn_sb=xn_sb):
                        for k in range(4):
                            hc = g * 4 + k
                            inst = e.transpose(
                                bk[:, k * 128:(k + 1) * 128],
                                xn_sb[:, hc * 128:(hc + 1) * 128], ident[:])
                        return inst
                    S.step("tensor", tr_group,
                           reads=[xn_buf, gb("ident")], writes=[bkB])
                    def evac(e, g=g, bpos=bpos):
                        src = bk[:].rearrange("p (k e) -> p k e", k=4)
                        dst = bass.AP(xnT, g * 4 * 512 + bpos * 128,
                                      [[NHC * 512, 128], [512, 4], [1, 128]])
                        return e.tensor_copy(dst, src)
                    S.step("vector", evac, reads=[bkB], writes=[xnTB])

            tap(f"xnT_s{s}", xnT[:], xnTB)
            for oc in range(24):
                wt = (d_wq[oc] if oc < 16 else
                      d_wk[oc - 16] if oc < 20 else d_wv[oc - 20])
                wpar = oc % 2
                for hhalf in range(2):
                    dma("sync", wbuf[wpar][:].rearrange("p (k e) -> p k e", k=8),
                        wt[hhalf * 1024:(hhalf + 1) * 1024, :]
                        .rearrange("(k p) e -> p k e", p=128),
                        writes=[wbB[wpar]] if hhalf == 0 else [wbB[wpar]])
                    def mmh(e, wpar=wpar, hhalf=hhalf):
                        for k in range(8):
                            hc = hhalf * 8 + k
                            inst = e.matmul(
                                pb[wpar][:],
                                wbuf[wpar][:, k * 128:(k + 1) * 128],
                                xnT[:, hc * 512:(hc + 1) * 512],
                                start=(hc == 0), stop=(hc == 15))
                        return inst
                    S.step("tensor", mmh,
                           reads=[wbB[wpar], xnTB], writes=[pbB[wpar]])
                if oc < 16:
                    proj_epilogue_qk(oc, pb[wpar], pbB[wpar], qT, oc * 512,
                                     qTB, cosb[par], sinb[par], cosB[par])
                elif oc < 20:
                    kvh = oc - 16
                    slot = par * 4 + kvh
                    proj_epilogue_qk(oc, pb[wpar], pbB[wpar], kT, slot * 512,
                                     kTB[slot], cosb[par], sinb[par],
                                     cosB[par])
                else:
                    kvh = oc - 20
                    slot = par * 4 + kvh
                    vf, vfB = scr[0], gb("scr0")
                    S.step("vector",
                           lambda e, wpar=wpar: e.tensor_copy(vf[:], pb[wpar][:]),
                           reads=[pbB[wpar]], writes=[vfB])
                    def vtr(e, vf=vf):
                        for k in range(4):
                            inst = e.transpose(pb[5][:, k * 128:(k + 1) * 128],
                                               vf[:, k * 128:(k + 1) * 128],
                                               ident[:])
                        return inst
                    S.step("tensor", vtr, reads=[vfB, gb("ident")],
                           writes=[pbB[5]])
                    S.step("vector",
                           lambda e, slot=slot: e.tensor_copy(
                               vC[:, slot * 512:(slot + 1) * 512], pb[5][:]),
                           reads=[pbB[5]], writes=[vCB[slot]])

            tap(f"qT_s{s}", qT[:], qTB)
            tap(f"kT_s{s}", kT[:], kTB)
            tap(f"vC_s{s}", vC[:], vCB)
            if s == 0:
                continue
            a = s - 1
            for half in range(2):
                for tih in range(2):
                    ti = 4 * a + 2 * half + tih
                    mpar = ti % 2
                    dma("sync", maskb[mpar][:].rearrange("p (k e) -> p k e", k=5),
                        d_mask[ti].transpose([1, 0, 2]), writes=[maskB[mpar]])
                    qcol = (ti - 4 * a) * 128
                    for h in range(NHEAD):
                        kvh = h // 4
                        def smm(e, ti=ti, h=h, kvh=kvh, qcol=qcol):
                            for kb in range(5):
                                ek = ti + kb
                                slot = ((ek // 4) % 2) * 4 + kvh
                                kap = kT[:, slot * 512 + (ek % 4) * 128:
                                         slot * 512 + (ek % 4) * 128 + 128]
                                inst = e.matmul(
                                    score_ps[:, kb * 128:(kb + 1) * 128],
                                    kap,
                                    qT[:, h * 512 + qcol:h * 512 + qcol + 128],
                                    start=(kb in (0, 4)), stop=(kb in (3, 4)))
                            return inst
                        ek_slots = [((ti + kb) // 4) % 2 * 4 + kvh
                                    for kb in range(5)]
                        S.step("tensor", smm,
                               reads=[qTB] + [kTB[sl] for sl in set(ek_slots)],
                               writes=[scoreB])
                        S.step("vector",
                               lambda e, mpar=mpar: e.scalar_tensor_tensor(
                                   prE[:], score_ps[:], SCALE, maskb[mpar][:],
                                   ALU.mult, ALU.add),
                               reads=[scoreB, maskB[mpar]], writes=[prEB])
                        S.step("scalar",
                               lambda e: e.activation(prE[:], prE[:], AF.Exp),
                               reads=[prEB], writes=[prEB])
                        def avz(e, ti=ti, kvh=kvh):
                            for kb in range(5):
                                ek = ti + kb
                                slot = ((ek // 4) % 2) * 4 + kvh
                                vap = vC[:, slot * 512 + (ek % 4) * 128:
                                         slot * 512 + (ek % 4) * 128 + 128]
                                e.matmul(pb[4][:, 0:128], vap,
                                         prE[:, kb * 128:(kb + 1) * 128],
                                         start=(kb == 0), stop=False)
                            for kb in range(5):
                                inst = e.matmul(
                                    pb[4][:, 128:256], ones[:],
                                    prE[:, kb * 128:(kb + 1) * 128],
                                    start=False, stop=(kb == 4))
                            return inst
                        vslots = [((ti + kb) // 4) % 2 * 4 + kvh
                                  for kb in range(5)]
                        S.step("tensor", avz,
                               reads=[prEB, gb("ones")] +
                                     [vCB[sl] for sl in set(vslots)],
                               writes=[pbB[4]])
                        tap(f"prE_t{ti}h{h}", prE[:], prEB)
                        S.step("vector",
                               lambda e: e.reciprocal(rz[:], pb[4][:, 128:256]),
                               reads=[pbB[4]], writes=[rzB])
                        ccol = (tih * NHC + h) * 128
                        S.step("vector",
                               lambda e, ccol=ccol: e.tensor_mul(
                                   concat[:, ccol:ccol + 128],
                                   pb[4][:, 0:128], rz[:]),
                               reads=[pbB[4], rzB], writes=[concatB])
                ti0 = 4 * a + 2 * half
                for tih in range(2):
                    ti = ti0 + tih
                    xs_sb, xs_buf = xring[ti % 3], xrB[ti % 3]
                    dma("sync", xs_sb[:],
                        d_xs[(ti + 4) * 128:(ti + 5) * 128, :],
                        writes=[xs_buf])
                for op_pair in range(2):
                    for hc in range(NHC):
                        wop = hc % 2
                        dma("sync", wob[wop][:],
                            d_wo[hc * 128:(hc + 1) * 128,
                                 op_pair * 1024:(op_pair + 1) * 1024],
                            writes=[woB[wop]])
                        def opmm(e, hc=hc, wop=wop):
                            for tih in range(2):
                                for occi in range(2):
                                    inst = e.matmul(
                                        pb[tih * 2 + occi][:],
                                        concat[:, (tih * NHC + hc) * 128:
                                               (tih * NHC + hc) * 128 + 128],
                                        wob[wop][:, occi * 512:
                                                 (occi + 1) * 512],
                                        start=(hc == 0), stop=(hc == NHC - 1))
                            return inst
                        S.step("tensor", opmm,
                               reads=[concatB, woB[wop]],
                               writes=[pbB[0], pbB[1], pbB[2], pbB[3]])
                    for tih in range(2):
                        ti = ti0 + tih
                        xs_sb, xs_buf = xring[ti % 3], xrB[ti % 3]
                        ost, ostB = pool2k[tih], p2kB[tih]
                        for occi in range(2):
                            col = (op_pair * 2 + occi) * 512
                            S.step("vector",
                                   lambda e, tih=tih, occi=occi, col=col,
                                          ost=ost, xs_sb=xs_sb: e.tensor_add(
                                       ost[:, col:col + 512],
                                       pb[tih * 2 + occi][:],
                                       xs_sb[:, col:col + 512]),
                                   reads=[pbB[tih * 2 + occi], xs_buf],
                                   writes=[ostB])
                for tih in range(2):
                    ti = ti0 + tih
                    dma("sync", d_out[ti * 128:(ti + 1) * 128, :],
                        pool2k[tih][:], reads=[p2kB[tih]],
                        writes=[gb("outrows")])



    if "B" in phases:
        S.barrier()
        idxB = gb("b_idx")
        dma("gpsimd", b_idx[:], d_didx.rearrange("(j p) -> p j", p=128),
            writes=[idxB], allow_slow_non_contiguous=True)
        bwB = [gb(f"b_w{i}") for i in range(2)]
        bw8B = [gb(f"b_w8{i}") for i in range(2)]
        bstB = [gb(f"b_st{i}") for i in range(2)]
        bmB = gb("b_m")
        wsgB, wsuB, ws2B = gb("b_wsg"), gb("b_wsu"), gb("b_ws2")
        scrB = {"g": gb("scr_g8"), "u": gb("scr_u8"), "d": gb("scr_d8")}

        for wname, d_src, d_scr, ws_sb, ws_buf in (
                ("g", d_wg, d_g8, b_wsg, wsgB), ("u", d_wu, d_u8, b_wsu, wsuB)):
            for j in range(NJ):
                par = j % 2
                mc = par * 2
                S.step("gpsimd",
                       lambda e, par=par, j=j, d_src=d_src:
                       e.indirect_dma_start(
                           out=b_w[par][:, 0:H], out_offset=None,
                           in_=d_src,
                           in_offset=IndirectOffsetOnAxis(b_idx[:, j:j + 1], 0)),
                       reads=[idxB], writes=[bwB[par]], dma=True)
                S.step("vector",
                       lambda e, par=par, mc=mc: e.reduce_max(
                           b_m[:, mc:mc + 1], b_w[par][:, 0:H], AX.X,
                           apply_absolute_value=True),
                       reads=[bwB[par]], writes=[bmB], narrow=True)
                S.step("vector",
                       lambda e, mc=mc: e.reciprocal(b_m[:, mc + 1:mc + 2],
                                                     b_m[:, mc:mc + 1]),
                       reads=[bmB], writes=[bmB], narrow=True)
                S.step("vector",
                       lambda e, mc=mc: e.tensor_scalar(
                           b_m[:, mc + 1:mc + 2], b_m[:, mc + 1:mc + 2],
                           224.0, None, ALU.mult),
                       reads=[bmB], writes=[bmB], narrow=True)
                S.step("vector",
                       lambda e, j=j, ws_sb=ws_sb, mc=mc: e.tensor_scalar(
                           ws_sb[:, j:j + 1], b_m[:, mc:mc + 1],
                           1.0 / 224.0, None, ALU.mult),
                       reads=[bmB], writes=[ws_buf], narrow=True)
                S.step("vector",
                       lambda e, par=par, mc=mc: e.tensor_scalar(
                           b_w8[par][:, 0:H], b_w[par][:, 0:H],
                           b_m[:, mc + 1:mc + 2], None, ALU.mult),
                       reads=[bwB[par], bmB], writes=[bw8B[par]])
                for grp in range(2):
                    bank = pb[4 + grp]
                    bankB = pbB[4 + grp]
                    def tr8(e, par=par, grp=grp):
                        bk8 = pb[4 + grp][:].bitcast(FP8)
                        for k in range(8):
                            c = grp * 8 + k
                            inst = e.transpose(
                                bk8[:, k * 256:(k + 1) * 256:2],
                                b_w8[par][:, c * 128:(c + 1) * 128],
                                ident8[:])
                        return inst
                    S.step("tensor", tr8,
                           reads=[bw8B[par], gb("ident8")], writes=[bankB])
                    def ev8(e, par=par, grp=grp):
                        src = pb[4 + grp][:].bitcast(FP8).rearrange(
                            "p (k s) -> p k s", k=8)[:, :, 0:256:2]
                        dst = b_stage[par][:, grp * 1024:(grp + 1) * 1024]\
                            .rearrange("p (k e) -> p k e", k=8)
                        return e.tensor_copy(dst, src)
                    S.step("vector", ev8, reads=[bankB], writes=[bstB[par]])
                dma("gpsimd", d_scr[j], b_stage[par][:, 0:H],
                    reads=[bstB[par]], writes=[scrB[wname]])
        for t16 in range(NHC):
            par = t16 % 2
            mc = par * 2
            dma("sync", b_w[par][:], d_wd[t16 * 128:(t16 + 1) * 128, :],
                writes=[bwB[par]])
            S.step("vector",
                   lambda e, par=par, mc=mc: e.reduce_max(
                       b_m[:, mc:mc + 1], b_w[par][:], AX.X,
                       apply_absolute_value=True),
                   reads=[bwB[par]], writes=[bmB], narrow=True)
            S.step("vector",
                   lambda e, mc=mc: e.reciprocal(b_m[:, mc + 1:mc + 2],
                                                 b_m[:, mc:mc + 1]),
                   reads=[bmB], writes=[bmB], narrow=True)
            S.step("vector",
                   lambda e, mc=mc: e.tensor_scalar(
                       b_m[:, mc + 1:mc + 2], b_m[:, mc + 1:mc + 2],
                       224.0, None, ALU.mult),
                   reads=[bmB], writes=[bmB], narrow=True)
            S.step("vector",
                   lambda e, t16=t16, mc=mc: e.tensor_scalar(
                       b_ws2[:, t16:t16 + 1], b_m[:, mc:mc + 1],
                       1.0 / 224.0, None, ALU.mult),
                   reads=[bmB], writes=[ws2B], narrow=True)
            S.step("vector",
                   lambda e, par=par, mc=mc: e.tensor_scalar(
                       b_w8[par][:], b_w[par][:],
                       b_m[:, mc + 1:mc + 2], None, ALU.mult),
                   reads=[bwB[par], bmB], writes=[bw8B[par]])
            for grp in range(6):
                npc = 8 if grp < 5 else 4
                bank = 4 + grp % 2
                def tr8d(e, par=par, grp=grp, npc=npc, bank=bank):
                    bk8 = pb[bank][:].bitcast(FP8)
                    for k in range(npc):
                        jc = grp * 8 + k
                        inst = e.transpose(
                            bk8[:, k * 256:(k + 1) * 256:2],
                            b_w8[par][:, jc * 128:(jc + 1) * 128],
                            ident8[:])
                    return inst
                S.step("tensor", tr8d,
                       reads=[bw8B[par], gb("ident8")], writes=[pbB[bank]])
                def ev8d(e, par=par, grp=grp, npc=npc, bank=bank):
                    src = pb[bank][:].bitcast(FP8).rearrange(
                        "p (k s) -> p k s", k=8)[:, 0:npc, 0:256:2]
                    dst = b_stage[par][:, grp * 1024:grp * 1024 + npc * 128]\
                        .rearrange("p (k e) -> p k e", k=npc)
                    return e.tensor_copy(dst, src)
                S.step("vector", ev8d, reads=[pbB[bank]], writes=[bstB[par]])
            dma("gpsimd", d_d8[t16], b_stage[par][:],
                reads=[bstB[par]], writes=[scrB["d"]])

    if "C" in phases:
        S.barrier()
        cxB = [gb(f"c_x{i}") for i in range(2)]
        cxnB = gb("c_xn")
        cstB = gb("c_stats")
        cq8B, cq8TB = gb("c_qx8"), gb("c_qx8T")
        cfxB, cfxbB = gb("c_fx"), gb("c_fxb")
        cwgB = [gb(f"c_wg8{i}") for i in range(2)]
        cwuB = [gb(f"c_wu8{i}") for i in range(2)]
        cwdB = [gb(f"c_wd8{i}") for i in range(2)]
        ctB, crmB, cqtB = gb("c_t"), gb("c_rmax"), gb("c_qt8")
        crmnB = gb("c_rmin")
        cgscB, cuscB, cgsiB = gb("c_gsc"), gb("c_usc"), gb("c_gsi")
        cqsbB, cftbB = gb("c_qsb"), gb("c_ftb")
        cy2B = [gb(f"c_y2{i}") for i in range(2)]
        costB = [gb(f"c_ost{i}") for i in range(2)]
        crowB, cmtB = gb("c_row"), gb("c_mrow")
        scrB2 = {"g": gb("scr_g8"), "u": gb("scr_u8"), "d": gb("scr_d8")}
        wsgB, wsuB, ws2B = gb("b_wsg"), gb("b_wsu"), gb("b_ws2")
        UTIL = pb[4]
        UTILB = pbB[4]
        UTIL2 = pb[5]
        UTIL2B = pbB[5]

        for cc in range(NCH):
            for bp in range(2):
                tb = cc * 2 + bp
                S_x, S_xB = c_x[bp], cxB[bp]
                dma("sync", S_x[:], d_xs[(tb + 4) * 128:(tb + 5) * 128, :],
                    writes=[S_xB])
                emit_ln_g(S_x, S_xB, c_xn, cxnB, c_ost[0], costB[0],
                          c_stats, cstB)
                S.step("vector",
                       lambda e: e.reduce_max(c_stats[:, 5:6], c_xn[:], AX.X,
                                              apply_absolute_value=True),
                       reads=[cxnB], writes=[cstB], narrow=True)
                S.step("vector",
                       lambda e: e.reciprocal(c_stats[:, 6:7], c_stats[:, 5:6]),
                       reads=[cstB], writes=[cstB], narrow=True)
                S.step("vector",
                       lambda e: e.tensor_scalar(c_stats[:, 6:7],
                                                 c_stats[:, 6:7], 224.0, None,
                                                 ALU.mult),
                       reads=[cstB], writes=[cstB], narrow=True)
                S.step("vector",
                       lambda e, bp=bp: e.tensor_scalar(
                           c_fx[:, bp:bp + 1], c_stats[:, 5:6],
                           1.0 / 224.0, None, ALU.mult),
                       reads=[cstB], writes=[cfxB], narrow=True)
                S.step("vector",
                       lambda e: e.tensor_scalar(c_qx8[:], c_xn[:],
                                                 c_stats[:, 6:7], None,
                                                 ALU.mult),
                       reads=[cxnB, cstB], writes=[cq8B])
                if cc == TAP_CC and bp == 0:
                    tap("c_xn0", c_xn[:], cxnB)
                    tap("c_qx80", c_qx8[:], cq8B)
                    tap("c_stats0", c_stats[:], cstB)
                for grp in range(2):
                    bank = (UTIL, UTIL2)[grp]
                    bankB = (UTILB, UTIL2B)[grp]
                    def trq(e, grp=grp, bank=bank):
                        bk8 = bank[:].bitcast(FP8)
                        for k in range(8):
                            hc = grp * 8 + k
                            inst = e.transpose(
                                bk8[:, k * 256:(k + 1) * 256:2],
                                c_qx8[:, hc * 128:(hc + 1) * 128], ident8[:])
                        return inst
                    S.step("tensor", trq, reads=[cq8B, gb("ident8")],
                           writes=[bankB])
                    def evq(e, grp=grp, bp=bp, bank=bank):
                        src = bank[:].bitcast(FP8).rearrange(
                            "p (k s) -> p k s", k=8)[:, :, 0:256:2]
                        dst = bass.AP(c_qx8T, (grp * 8) * CC + bp * 128,
                                      [[NHC * CC, 128], [CC, 8], [1, 128]])
                        return e.tensor_copy(dst, src)
                    S.step("vector", evq, reads=[bankB], writes=[cq8TB])
                S.step("tensor",
                       lambda e, bp=bp: e.matmul(
                           pb[0][0:1, bp * 128:(bp + 1) * 128],
                           c_fx[:, bp:bp + 1], ident[:],
                           start=(bp == 0), stop=(bp == 1)),
                       reads=[cfxB, gb("ident")], writes=[pbB[0]])
            S.step("vector",
                   lambda e: e.tensor_copy(c_row[0:1, 0:CC], pb[0][0:1, 0:CC]),
                   reads=[pbB[0]], writes=[crowB], narrow=True)
            S.step("tensor",
                   lambda e: e.matmul(UTIL2[:, 0:CC], ones[0:1, :],
                                      c_row[0:1, 0:CC], start=True, stop=True),
                   reads=[crowB, gb("ones")], writes=[UTIL2B])
            S.step("vector",
                   lambda e: e.tensor_copy(c_fxb[:], UTIL2[:, 0:CC]),
                   reads=[UTIL2B], writes=[cfxbB])
            if cc == TAP_CC:
                tap("c_qx8T0", c_qx8T[:], cq8TB)
                tap("c_fxb0", c_fxb[:], cfxbB)
            for j in range(NJ):
                par = j % 2
                pg, pgB = pb[2 * par], pbB[2 * par]
                pu, puB = pb[2 * par + 1], pbB[2 * par + 1]
                dma("gpsimd", c_wg8[par][:], d_g8[j], writes=[cwgB[par]])
                dma("gpsimd", c_wu8[par][:], d_u8[j], writes=[cwuB[par]])
                def gmm(e, par=par, pg=pg):
                    for hc in range(NHC):
                        inst = e.matmul(pg[:, 0:CC],
                                        c_wg8[par][:, hc * 128:(hc + 1) * 128],
                                        c_qx8T[:, hc * CC:(hc + 1) * CC],
                                        start=(hc == 0), stop=(hc == NHC - 1))
                    return inst
                S.step("tensor", gmm, reads=[cwgB[par], cq8TB], writes=[pgB])
                def umm(e, par=par, pu=pu):
                    for hc in range(NHC):
                        inst = e.matmul(pu[:, 0:CC],
                                        c_wu8[par][:, hc * 128:(hc + 1) * 128],
                                        c_qx8T[:, hc * CC:(hc + 1) * CC],
                                        start=(hc == 0), stop=(hc == NHC - 1))
                    return inst
                S.step("tensor", umm, reads=[cwuB[par], cq8TB], writes=[puB])
                S.step("vector",
                       lambda e, j=j, pg=pg: e.scalar_tensor_tensor(
                           c_gsc[:], pg[:, 0:CC], b_wsg[:, j:j + 1], c_fxb[:],
                           ALU.mult, ALU.mult),
                       reads=[pgB, wsgB, cfxbB], writes=[cgscB])
                S.step("scalar",
                       lambda e: e.activation(c_gsi[:], c_gsc[:], AF.Silu),
                       reads=[cgscB], writes=[cgsiB])
                S.step("vector",
                       lambda e, j=j, pu=pu: e.scalar_tensor_tensor(
                           c_usc[:], pu[:, 0:CC], b_wsu[:, j:j + 1], c_fxb[:],
                           ALU.mult, ALU.mult),
                       reads=[puB, wsuB, cfxbB], writes=[cuscB])
                if cc == TAP_CC and j == TAP_J:
                    tap("c_gsc0", c_gsc[:], cgscB)
                    tap("c_usc0", c_usc[:], cuscB)
                    tap("c_wg80", c_wg8[j % 2][:], cwgB[j % 2])
                S.step("gpsimd",
                       lambda e, j=j: e.tensor_mul(
                           c_t[:, j * CC:(j + 1) * CC], c_gsi[:], c_usc[:]),
                       reads=[cgsiB, cuscB], writes=[ctB])
                if j == 0:
                    S.step("vector",
                           lambda e: e.tensor_copy(c_rmax[:], c_t[:, 0:CC]),
                           reads=[ctB], writes=[crmB])
                    S.step("vector",
                           lambda e: e.tensor_copy(c_rmin[:], c_t[:, 0:CC]),
                           reads=[ctB], writes=[crmnB])
                else:
                    S.step("vector",
                           lambda e, j=j: e.tensor_max(
                               c_rmax[:], c_rmax[:],
                               c_t[:, j * CC:(j + 1) * CC]),
                           reads=[ctB, crmB], writes=[crmB])
                    S.step("vector",
                           lambda e, j=j: e.tensor_tensor(
                               c_rmin[:], c_rmin[:],
                               c_t[:, j * CC:(j + 1) * CC], ALU.min),
                           reads=[ctB, crmnB], writes=[crmnB])
            S.step("vector",
                   lambda e: e.tensor_scalar(c_rmin[:], c_rmin[:], -1.0,
                                             None, ALU.mult),
                   reads=[crmnB], writes=[crmnB])
            S.step("vector",
                   lambda e: e.tensor_max(c_rmax[:], c_rmax[:], c_rmin[:]),
                   reads=[crmnB, crmB], writes=[crmB])
            if cc == TAP_CC:
                tap("c_t0", c_t[:], ctB)
                tap("c_rmax0", c_rmax[:], crmB)
                tap("b_wsg0", b_wsg[:], wsgB)
                tap("b_ws20", b_ws2[:], ws2B)
            for bp in range(2):
                S.step("tensor",
                       lambda e, bp=bp: e.transpose(
                           UTIL[:, bp * 128:(bp + 1) * 128],
                           c_rmax[:, bp * 128:(bp + 1) * 128], ident[:]),
                       reads=[crmB, gb("ident")], writes=[UTILB])
                S.step("vector",
                       lambda e, bp=bp: e.reduce_max(
                           c_mrow[:, bp:bp + 1],
                           UTIL[:, bp * 128:(bp + 1) * 128], AX.X),
                       reads=[UTILB], writes=[cmtB], narrow=True)
                S.step("tensor",
                       lambda e, bp=bp: e.matmul(
                           UTIL2[0:1, bp * 128:(bp + 1) * 128],
                           c_mrow[:, bp:bp + 1], ident[:],
                           start=(bp == 0), stop=(bp == 1)),
                       reads=[cmtB, gb("ident")], writes=[UTIL2B])
            S.step("vector",
                   lambda e: e.tensor_copy(c_row[0:1, 256:256 + CC],
                                           UTIL2[0:1, 0:CC]),
                   reads=[UTIL2B], writes=[crowB], narrow=True)
            S.step("vector",
                   lambda e: e.reciprocal(c_row[0:1, 0:CC],
                                          c_row[0:1, 256:256 + CC]),
                   reads=[crowB], writes=[crowB], narrow=True)
            S.step("vector",
                   lambda e: e.tensor_scalar(c_row[0:1, 0:CC],
                                             c_row[0:1, 0:CC], 224.0, None,
                                             ALU.mult),
                   reads=[crowB], writes=[crowB], narrow=True)
            S.step("vector",
                   lambda e: e.tensor_scalar(c_row[0:1, 256:256 + CC],
                                             c_row[0:1, 256:256 + CC],
                                             1.0 / 224.0, None, ALU.mult),
                   reads=[crowB], writes=[crowB], narrow=True)
            S.step("tensor",
                   lambda e: e.matmul(UTIL[:, 0:CC], ones[0:1, :],
                                      c_row[0:1, 0:CC], start=True, stop=True),
                   reads=[crowB, gb("ones")], writes=[UTILB])
            S.step("vector", lambda e: e.tensor_copy(c_qsb[:], UTIL[:, 0:CC]),
                   reads=[UTILB], writes=[cqsbB])
            S.step("tensor",
                   lambda e: e.matmul(UTIL[:, 0:CC], ones[0:1, :],
                                      c_row[0:1, 256:256 + CC],
                                      start=True, stop=True),
                   reads=[crowB, gb("ones")], writes=[UTILB])
            S.step("vector", lambda e: e.tensor_copy(c_ftb[:], UTIL[:, 0:CC]),
                   reads=[UTILB], writes=[cftbB])
            if cc == TAP_CC:
                tap("c_qsb0", c_qsb[:], cqsbB)
                tap("c_ftb0", c_ftb[:], cftbB)
            for j in range(NJ):
                S.step("vector",
                       lambda e, j=j: e.tensor_tensor(
                           c_qt8[:, j * CC:(j + 1) * CC],
                           c_t[:, j * CC:(j + 1) * CC], c_qsb[:], ALU.mult),
                       reads=[ctB, cqsbB], writes=[cqtB])
            if cc == TAP_CC:
                tap("c_qt80", c_qt8[:], cqtB)
            for hc in range(NHC):
                par = hc % 2
                pd, pdB = pb[2 + par], pbB[2 + par]
                dma("gpsimd", c_wd8[par][:], d_d8[hc], writes=[cwdB[par]])
                def dmm(e, par=par, pd=pd):
                    for jc in range(NJ):
                        inst = e.matmul(pd[:, 0:CC],
                                        c_wd8[par][:, jc * 128:(jc + 1) * 128],
                                        c_qt8[:, jc * CC:(jc + 1) * CC],
                                        start=(jc == 0), stop=(jc == NJ - 1))
                    return inst
                S.step("tensor", dmm, reads=[cwdB[par], cqtB], writes=[pdB])
                S.step("vector",
                       lambda e, hc=hc, par=par, pd=pd: e.scalar_tensor_tensor(
                           c_y2[par][:, 0:CC], pd[:, 0:CC],
                           b_ws2[:, hc:hc + 1], c_ftb[:], ALU.mult, ALU.mult),
                       reads=[pdB, ws2B, cftbB], writes=[cy2B[par]])
                for bp in range(2):
                    S.step("tensor",
                           lambda e, hc=hc, par=par, bp=bp: e.transpose(
                               UTIL[:, bp * 128:(bp + 1) * 128],
                               c_y2[par][:, bp * 128:(bp + 1) * 128],
                               ident[:]),
                           reads=[cy2B[par], gb("ident")], writes=[UTILB])
                    S.step("vector",
                           lambda e, hc=hc, bp=bp: e.tensor_copy(
                               c_ost[bp][:, hc * 128:(hc + 1) * 128],
                               UTIL[:, bp * 128:(bp + 1) * 128]),
                           reads=[UTILB], writes=[costB[bp]])
            for bp in range(2):
                tb = cc * 2 + bp
                dma("gpsimd", d_out[tb * 128:(tb + 1) * 128, :],
                    c_ost[bp][:], accum_op=ALU.add,
                    reads=[costB[bp]], writes=[gb("outrows")])

    S.barrier()
    final_waits = [(n, c) for n, c in S.cnt.items() if n.startswith("q_")]
    S.step("gpsimd", lambda e: e.nop(), extra_waits=final_waits)

    sems = {}
    from contextlib import ExitStack
    with ExitStack() as stack:
        for name in S.cnt:
            sems[name] = stack.enter_context(nc.semaphore(name))
        S.finalize(nc, sems)
    return nc


def assemble(results):
    out = np.empty((BATCH, T, H), dtype=np.float32)
    for c in range(NCORE):
        b, hh = c // 2, c % 2
        out[b, hh * TOK:(hh + 1) * TOK] = results[c]["out"]
    return out


def kernel(**inputs):
    in_maps = stage_inputs(**inputs)
    nc = build_program()
    res = run_bass_kernel_spmd(nc, in_maps, core_ids=list(range(NCORE)))
    return assemble(res.results)


# version 02
# speedup vs baseline: 1.4670x; 1.4670x over previous
import numpy as np
import concourse.bass as bass
import concourse.mybir as mybir
from concourse.bass_utils import run_bass_kernel_spmd
from concourse.bass_core_rust import IndirectOffsetOnAxis

F32 = mybir.dt.float32
BF16 = mybir.dt.bfloat16
FP8 = mybir.dt.float8e4
U32 = mybir.dt.uint32
AF = mybir.ActivationFunctionType
ALU = mybir.AluOpType
AX = mybir.AxisListType

BATCH, T, H = 4, 4096, 2048
NCORE = 8
TOK = 2048
HALO = 512
TE = TOK + HALO
NB = TE // 128
NQ = TOK // 128
NSLAB = NB // 4
D, NHEAD, NKVH = 128, 16, 4
NHC = H // 128
II = 5632
NJ = II // 128
CC = 256
NCH = TOK // CC
SCALE = float(1.0 / np.sqrt(128.0))
NEGINF = -1.0e30
LN_EPS = 1e-5
TAP_CC = 0
TAP_J = 9

ENGINES = ("sync", "gpsimd", "tensor", "vector", "scalar")


class Buf:
    __slots__ = ("ready", "reads")

    def __init__(self):
        self.ready = None
        self.reads = []


class Sched:

    def __init__(self):
        self.prog = {e: [] for e in ENGINES}
        self.cnt = {}
        self.hwm = {e: {} for e in ENGINES}

    def step(self, eng, fn, reads=(), writes=(), sem=None, dma=False,
             extra_waits=(), narrow=False):
        if sem is None:
            sem = ("q_" + eng) if dma else ("c_" + eng)
        amt = 16 if dma else 1
        waits = {}

        def addw(dep):
            peng, psem, pcnt, upg = dep
            if peng != eng:
                if upg:
                    pcnt = self.cnt[psem]
                waits[psem] = max(waits.get(psem, 0), pcnt)

        for b in reads:
            if b.ready is not None:
                addw(b.ready)
        for b in writes:
            if b.ready is not None:
                addw(b.ready)
            for dep in b.reads:
                addw(dep)
        for psem, pcnt in extra_waits:
            waits[psem] = max(waits.get(psem, 0), pcnt)
        hwm = self.hwm[eng]
        wl = []
        for n in sorted(waits):
            v = waits[n]
            if v > hwm.get(n, 0):
                wl.append((n, v))
                hwm[n] = v
        self.cnt[sem] = self.cnt.get(sem, 0) + amt
        upg = dma and eng == "gpsimd"
        me = ((None if dma else eng), sem, self.cnt[sem], upg)
        for b in writes:
            b.ready = me
            b.reads = []
        for b in reads:
            b.reads.append(me)
        if narrow:
            inner = fn

            def fn(e, _inner=inner):
                _inner(e)
                return e.tensor_copy(self.junk[1][:], self.junk[0][:])
        self.prog[eng].append((wl, fn, sem, amt))
        return me

    def barrier(self):
        snap = [(n, c) for n, c in self.cnt.items()]
        for eng in ENGINES:
            self.step(eng, lambda e: e.nop(), extra_waits=snap)

    def finalize(self, nc, sems):
        with nc.Block() as blk:
            for eng_name in ENGINES:
                def body(e, _eng=eng_name):
                    for wl, fn, sem, amt in self.prog[_eng]:
                        for n, v in wl:
                            e.wait_ge(sems[n], v)
                        inst = fn(e)
                        if inst is None:
                            inst = e.nop()
                        inst.then_inc(sems[sem], amt)
                getattr(blk, eng_name)(body)



def _rope_tables():
    inv = 1.0 / (10000.0 ** (np.arange(0, D, 2, dtype=np.float64) / D))
    tabs = []
    for hh in (0, 1):
        pos = np.arange(hh * TOK - HALO, hh * TOK + TOK, dtype=np.float64)
        fr = pos[:, None] * inv[None, :]
        cos = np.concatenate([np.cos(fr), np.cos(fr)], axis=1)
        sin = np.concatenate([np.sin(fr), np.sin(fr)], axis=1)
        tabs.append((np.ascontiguousarray(cos.T.astype(np.float32)),
                     np.ascontiguousarray(sin.T.astype(np.float32))))
    return tabs


def _masks():
    out = []
    for hh in (0, 1):
        m = np.full((NQ, 5, 128, 128), NEGINF, dtype=np.float32)
        for ti in range(NQ):
            q = hh * TOK + ti * 128 + np.arange(128)
            for kb in range(5):
                k = hh * TOK - HALO + (ti + kb) * 128 + np.arange(128)
                dif = q[None, :] - k[:, None]
                ok = (dif >= 0) & (dif < HALO) & (k[:, None] >= 0)
                m[ti, kb][ok] = 0.0
        out.append(m)
    return out


def _rot_lhsT():
    M = np.zeros((128, 128), dtype=np.float32)
    for d in range(64):
        M[d, d + 64] = -1.0
        M[d + 64, d] = 1.0
    return np.ascontiguousarray(M.T)


def stage_inputs(L0_attn_x, L0_attn_wq, L0_attn_wk, L0_attn_wv, L0_attn_wo,
                 L0_ffn_w_gate, L0_ffn_w_up, L0_ffn_w_down,
                 L0_ffn_up_reorder_index, L0_ffn_down_reorder_index):
    x = np.asarray(L0_attn_x, dtype=np.float32)
    wq = np.ascontiguousarray(np.asarray(L0_attn_wq, np.float32)
                              .reshape(H, NHEAD, 128).transpose(1, 0, 2))
    wk = np.ascontiguousarray(np.asarray(L0_attn_wk, np.float32)
                              .reshape(H, NKVH, 128).transpose(1, 0, 2))
    wv = np.ascontiguousarray(np.asarray(L0_attn_wv, np.float32)
                              .reshape(H, NKVH, 128).transpose(1, 0, 2))
    wo = np.ascontiguousarray(np.asarray(L0_attn_wo, np.float32))
    wg = np.ascontiguousarray(np.asarray(L0_ffn_w_gate, np.float32))
    wu = np.ascontiguousarray(np.asarray(L0_ffn_w_up, np.float32))
    wd = np.ascontiguousarray(np.asarray(L0_ffn_w_down, np.float32))
    didx = np.ascontiguousarray(
        np.asarray(L0_ffn_down_reorder_index).astype(np.uint32))
    rope = _rope_tables()
    masks = _masks()
    rmat = _rot_lhsT()
    ident = np.ascontiguousarray(np.eye(128, dtype=np.float32))
    ones = np.ones((128, 128), dtype=np.float32)
    consts = np.zeros((128, 2), dtype=np.float32)
    consts[:, 0] = LN_EPS
    consts[:, 1] = 1.0 / H
    in_maps = []
    for c in range(NCORE):
        b, hh = c // 2, c % 2
        xs = np.zeros((TE, H), dtype=np.float32)
        lo = hh * TOK - HALO
        src_lo = max(lo, 0)
        xs[src_lo - lo:] = x[b, src_lo:hh * TOK + TOK]
        cosT, sinT = rope[hh]
        in_maps.append({
            "xs": np.ascontiguousarray(xs),
            "wq": wq, "wk": wk, "wv": wv, "wo": wo,
            "wg": wg, "wu": wu, "wd": wd, "didx": didx,
            "rcosT": cosT, "rsinT": sinT, "maskT": masks[hh],
            "rmat": rmat, "ident": ident, "ones": ones, "consts": consts,
        })
    return in_maps



def build_program(phases=("A", "B", "C"), taps=()):
    nc = bass.Bass("TRN2", target_bir_lowering=False, debug=False)
    S = Sched()

    d_xs = nc.dram_tensor("xs", [TE, H], F32, kind="ExternalInput").ap()
    d_wq = nc.dram_tensor("wq", [NHEAD, H, 128], F32, kind="ExternalInput").ap()
    d_wk = nc.dram_tensor("wk", [NKVH, H, 128], F32, kind="ExternalInput").ap()
    d_wv = nc.dram_tensor("wv", [NKVH, H, 128], F32, kind="ExternalInput").ap()
    d_wo = nc.dram_tensor("wo", [H, H], F32, kind="ExternalInput").ap()
    d_wg = nc.dram_tensor("wg", [II, H], F32, kind="ExternalInput").ap()
    d_wu = nc.dram_tensor("wu", [II, H], F32, kind="ExternalInput").ap()
    d_wd = nc.dram_tensor("wd", [H, II], F32, kind="ExternalInput").ap()
    d_didx = nc.dram_tensor("didx", [II], U32, kind="ExternalInput").ap()
    d_cos = nc.dram_tensor("rcosT", [128, TE], F32, kind="ExternalInput").ap()
    d_sin = nc.dram_tensor("rsinT", [128, TE], F32, kind="ExternalInput").ap()
    d_mask = nc.dram_tensor("maskT", [NQ, 5, 128, 128], F32,
                            kind="ExternalInput").ap()
    d_rmat = nc.dram_tensor("rmat", [128, 128], F32, kind="ExternalInput").ap()
    d_ident = nc.dram_tensor("ident", [128, 128], F32, kind="ExternalInput").ap()
    d_ones = nc.dram_tensor("ones", [128, 128], F32, kind="ExternalInput").ap()
    d_consts = nc.dram_tensor("consts", [128, 2], F32, kind="ExternalInput").ap()
    d_out = nc.dram_tensor("out", [TOK, H], F32, kind="ExternalOutput").ap()
    d_g8 = nc.dram_tensor("scr_g8", [NJ, 128, H], FP8, kind="Internal").ap()
    d_u8 = nc.dram_tensor("scr_u8", [NJ, 128, H], FP8, kind="Internal").ap()
    d_d8 = nc.dram_tensor("scr_d8", [NHC, 128, II], FP8, kind="Internal").ap()
    taps = set(taps)

    def tap(name, sb_ap, buf, eng="gpsimd"):
        if name not in taps:
            return
        shape = list(sb_ap.shape)
        d = nc.dram_tensor("tap_" + name, shape, sb_ap.dtype,
                           kind="ExternalOutput").ap()
        bl = buf if isinstance(buf, (list, tuple)) else [buf]
        dma(eng, d, sb_ap, reads=bl, writes=[gb("tapw")])

    sb_base = (nc._sbuf_addr_for_side("left") + 31) & ~31
    sb_limit = sb_base + nc.sbuf_bytes_remaining - 64
    off = [sb_base]
    sb = {}

    def alloc(name, shape, dtype, at=None):
        per = int(np.prod(shape[1:])) * mybir.dt.size(dtype)
        if at is None:
            at = off[0]
            off[0] = (at + per + 31) & ~31
        h = nc.alloc_sbuf_tensor_at(name, shape, dtype, offset=at)
        sb[name] = h
        return h

    consts = alloc("c_consts", [128, 2], F32)
    ident = alloc("c_ident", [128, 128], F32)
    ident8 = alloc("c_ident8", [128, 128], FP8)
    rmat = alloc("c_rmat", [128, 128], F32)
    ones = alloc("c_ones", [128, 128], F32)
    junk0 = alloc("c_junk0", [128, 128], F32)
    junk1 = alloc("c_junk1", [128, 128], F32)
    S.junk = (junk0, junk1)
    common_end = off[0]

    off[0] = common_end
    cosb = [alloc(f"a_cos{i}", [128, 512], F32) for i in range(2)]
    sinb = [alloc(f"a_sin{i}", [128, 512], F32) for i in range(2)]
    xring = [alloc(f"a_x{i}", [128, H], F32) for i in range(3)]
    pool2k = [alloc(f"a_p2k{i}", [128, H], F32) for i in range(3)]
    xnT = alloc("a_xnT", [128, NHC * 512], F32)
    wbuf = [alloc(f"a_wb{i}", [128, 8 * 128], F32) for i in range(2)]
    qT = alloc("a_qT", [128, NHEAD * 512], F32)
    kT = alloc("a_kT", [128, 2 * NKVH * 512], F32)
    vC = alloc("a_vC", [128, 2 * NKVH * 512], F32)
    scr = [alloc(f"a_scr{i}", [128, 512], F32) for i in range(5)]
    maskb = [alloc(f"a_mb{i}", [128, 640], F32) for i in range(2)]
    prE = alloc("a_prE", [128, 640], F32)
    prE2 = alloc("a_prE2", [128, 640], F32)
    rz = alloc("a_rz", [128, 128], F32)
    rz2 = alloc("a_rz2", [128, 128], F32)
    concat = alloc("a_concat", [128, 2 * NHC * 128], F32)
    wob = [alloc(f"a_wo{i}", [128, 1024], F32) for i in range(2)]
    a_end = off[0]
    assert a_end <= sb_limit, f"phase A SBUF over: {a_end - sb_base}"

    off[0] = common_end
    b_w = [alloc(f"b_w{i}", [128, II], F32) for i in range(2)]
    b_w8 = [alloc(f"b_w8{i}", [128, II], FP8) for i in range(2)]
    b_stage = [alloc(f"b_st{i}", [128, II], FP8) for i in range(2)]
    b_idx = alloc("b_idx", [128, NJ], U32)
    b_wsg = alloc("b_wsg", [128, NJ], F32)
    b_wsu = alloc("b_wsu", [128, NJ], F32)
    b_ws2 = alloc("b_ws2", [128, NHC], F32)
    b_m = alloc("b_m", [128, 4], F32)
    b_end = off[0]
    assert b_end <= sb_limit, f"phase B SBUF over: {b_end - sb_base}"

    b_keep_end = b_end
    off[0] = b_keep_end
    c_x = [alloc(f"c_x{i}", [128, H], F32) for i in range(2)]
    c_xn = alloc("c_xn", [128, H], F32)
    c_qx8 = alloc("c_qx8", [128, H], FP8)
    c_qx8T = alloc("c_qx8T", [128, NHC * CC], FP8)
    c_fx = alloc("c_fx", [128, 2], F32)
    c_fxb = alloc("c_fxb", [128, CC], F32)
    c_wg8 = [alloc(f"c_wg8{i}", [128, H], FP8) for i in range(2)]
    c_wu8 = [alloc(f"c_wu8{i}", [128, H], FP8) for i in range(2)]
    c_t = alloc("c_t", [128, NJ * CC], F32)
    c_rmax = alloc("c_rmax", [128, CC], F32)
    c_rmin = alloc("c_rmin", [128, CC], F32)
    c_qt8 = alloc("c_qt8", [128, NJ * CC], FP8)
    c_wd8 = [alloc(f"c_wd8{i}", [128, II], FP8) for i in range(2)]
    c_qsb = alloc("c_qsb", [128, CC], F32)
    c_ftb = alloc("c_ftb", [128, CC], F32)
    c_y2 = [alloc(f"c_y2{i}", [128, 512], F32) for i in range(2)]
    c_ost = [alloc(f"c_ost{i}", [128, H], F32) for i in range(2)]
    c_mrow = alloc("c_mrow", [128, 8], F32)
    c_stats = alloc("c_stats", [128, 8], F32)
    c_gsc = alloc("c_gsc", [128, CC], F32)
    c_usc = alloc("c_usc", [128, CC], F32)
    c_gsi = alloc("c_gsi", [128, CC], F32)
    c_row = alloc("c_row", [128, 512], F32)
    c_end = off[0]
    assert c_end <= sb_limit, f"phase C SBUF over: {c_end - sb_base}"

    pb = [nc.alloc_psum_tensor(f"pb{i}", [128, 512], F32) for i in range(6)]
    score_ps = nc.alloc_psum_tensor("score_ps", [128, 640], F32)

    bufs = {}

    def gb(name):
        if name not in bufs:
            bufs[name] = Buf()
        return bufs[name]

    pbB = [gb(f"pb{i}") for i in range(6)]
    scoreB = gb("score")

    def dma(eng, out_ap, in_ap, reads=(), writes=(), **kw):
        return S.step(eng, lambda e: e.dma_start(out=out_ap, in_=in_ap, **kw),
                      reads=reads, writes=writes, dma=True)

    cB = gb("consts")
    dma("sync", consts[:], d_consts, writes=[cB])
    dma("sync", ident[:], d_ident, writes=[gb("ident")])
    dma("sync", rmat[:], d_rmat, writes=[gb("rmat")])
    dma("sync", ones[:], d_ones, writes=[gb("ones")])
    S.step("vector", lambda e: e.tensor_copy(ident8[:], ident[:]),
           reads=[gb("ident")], writes=[gb("ident8")])

    EPS = consts[:, 0:1]
    INVH = consts[:, 1:2]

    def emit_ln_g(xsb, xbuf, xnsb, xnbuf, lnscr, lnscrB, stats, statsB):
        S.step("scalar",
               lambda e: e.activation(lnscr[:], xsb[:], AF.Copy,
                                      accum_out=stats[:, 0:1]),
               reads=[xbuf], writes=[lnscrB, statsB])
        S.step("vector",
               lambda e: e.tensor_scalar(stats[:, 1:2], stats[:, 0:1],
                                         -1.0 / H, None, ALU.mult),
               reads=[statsB], writes=[statsB])
        S.step("scalar",
               lambda e: e.activation(xnsb[:], xsb[:], AF.Identity,
                                      bias=stats[:, 1:2]),
               reads=[xbuf, statsB], writes=[xnbuf])
        S.step("vector",
               lambda e: e.scalar_tensor_tensor(
                   lnscr[:], xnsb[:], 1.0, xnsb[:], ALU.bypass, ALU.mult,
                   accum_out=stats[:, 2:3]),
               reads=[xnbuf], writes=[lnscrB, statsB])
        S.step("scalar",
               lambda e: e.activation(stats[:, 3:4], stats[:, 2:3], AF.Sqrt,
                                      bias=EPS, scale=INVH),
               reads=[statsB, cB], writes=[statsB])
        S.step("vector",
               lambda e: e.reciprocal(stats[:, 4:5], stats[:, 3:4]),
               reads=[statsB], writes=[statsB])
        S.step("scalar",
               lambda e: e.activation(xnsb[:], xnsb[:], AF.Copy,
                                      scale=stats[:, 4:5]),
               reads=[statsB, xnbuf], writes=[xnbuf])

    if "A" in phases:
        lnscr = pool2k[2]
        lnscrB = gb("lnscr")
        stats = scr[4]
        statsB = gb("stats")

        def emit_ln(xsb, xbuf, xnsb, xnbuf):
            emit_ln_g(xsb, xbuf, xnsb, xnbuf, lnscr, lnscrB, stats, statsB)

        xnTB = gb("xnT")
        qTB = gb("qT")
        kTB = [gb(f"kT{i}") for i in range(8)]
        vCB = [gb(f"vC{i}") for i in range(8)]
        cosB = [gb(f"cos{i}") for i in range(2)]
        maskB = [gb(f"mask{i}") for i in range(2)]
        concatB = gb("concat")
        prEB = gb("prE")
        prE2B = gb("prE2")
        rzB = gb("rz")
        rz2B = gb("rz2")
        woB = [gb(f"wo{i}") for i in range(2)]
        wbB = [gb(f"wb{i}") for i in range(2)]
        xrB = [gb(f"xr{i}") for i in range(3)]
        p2kB = [gb(f"p2k{i}") for i in range(3)]

        def proj_epilogue_qk(oc, bank, bank_buf, dst_sb, dst_col, dst_buf,
                             cos_sb, sin_sb, cs_buf):
            qsb, qsbB = scr[0], gb("scr0")
            sq, sqB = scr[1], gb("scr1")
            r_, rB = scr[2], gb("scr2")
            a_, aB = scr[3], gb("scr3")
            S.step("scalar", lambda e: e.activation(qsb[:], bank[:], AF.Copy),
                   reads=[bank_buf], writes=[qsbB])
            S.step("scalar", lambda e: e.activation(sq[:], bank[:], AF.Square),
                   reads=[bank_buf], writes=[sqB])
            S.step("tensor",
                   lambda e: e.matmul(pb[2][:], ones[:], sq[:],
                                      start=True, stop=True),
                   reads=[sqB, gb("ones")], writes=[pbB[2]])
            S.step("tensor",
                   lambda e: e.matmul(pb[3][:], rmat[:], qsb[:],
                                      start=True, stop=True),
                   reads=[qsbB, gb("rmat")], writes=[pbB[3]])
            S.step("scalar", lambda e: e.activation(r_[:], pb[2][:], AF.Sqrt),
                   reads=[pbB[2]], writes=[rB])
            S.step("vector",
                   lambda e: e.tensor_scalar(r_[:], r_[:], 1e-12, None,
                                             ALU.max),
                   reads=[rB], writes=[rB])
            S.step("vector", lambda e: e.reciprocal(r_[:], r_[:]),
                   reads=[rB], writes=[rB])
            S.step("gpsimd", lambda e: e.tensor_mul(a_[:], qsb[:], cos_sb[:]),
                   reads=[qsbB, cs_buf], writes=[aB])
            S.step("vector",
                   lambda e: e.scalar_tensor_tensor(
                       sq[:], pb[3][:], 1.0, sin_sb[:], ALU.bypass, ALU.mult),
                   reads=[pbB[3], cs_buf], writes=[sqB])
            S.step("vector", lambda e: e.tensor_add(a_[:], a_[:], sq[:]),
                   reads=[sqB, aB], writes=[aB])
            S.step("vector",
                   lambda e: e.tensor_mul(dst_sb[:, dst_col:dst_col + 512],
                                          a_[:], r_[:]),
                   reads=[aB, rB], writes=[dst_buf])

        for s in range(NSLAB):
            par = s % 2
            dma("sync", cosb[par][:], d_cos[:, s * 512:(s + 1) * 512],
                writes=[cosB[par]])
            dma("sync", sinb[par][:], d_sin[:, s * 512:(s + 1) * 512],
                writes=[gb(f"sin{par}")])
            for bpos in range(4):
                eb = 4 * s + bpos
                xs_sb, xs_buf = xring[eb % 3], xrB[eb % 3]
                dma("sync", xs_sb[:], d_xs[eb * 128:(eb + 1) * 128, :],
                    writes=[xs_buf])
                xn_sb, xn_buf = pool2k[eb % 2], p2kB[eb % 2]
                emit_ln(xs_sb, xs_buf, xn_sb, xn_buf)
                tap(f"xn_e{eb}", xn_sb[:], xn_buf)
                for g in range(4):
                    bk, bkB = pb[4], pbB[4]
                    def tr_group(e, g=g, xn_sb=xn_sb):
                        for k in range(4):
                            hc = g * 4 + k
                            inst = e.transpose(
                                bk[:, k * 128:(k + 1) * 128],
                                xn_sb[:, hc * 128:(hc + 1) * 128], ident[:])
                        return inst
                    S.step("tensor", tr_group,
                           reads=[xn_buf, gb("ident")], writes=[bkB])
                    def evac(e, g=g, bpos=bpos):
                        src = bk[:].rearrange("p (k e) -> p k e", k=4)
                        dst = bass.AP(xnT, g * 4 * 512 + bpos * 128,
                                      [[NHC * 512, 128], [512, 4], [1, 128]])
                        return e.tensor_copy(dst, src)
                    S.step("vector", evac, reads=[bkB], writes=[xnTB])

            tap(f"xnT_s{s}", xnT[:], xnTB)
            for oc in range(24):
                wt = (d_wq[oc] if oc < 16 else
                      d_wk[oc - 16] if oc < 20 else d_wv[oc - 20])
                wpar = oc % 2
                for hhalf in range(2):
                    dma("sync", wbuf[wpar][:].rearrange("p (k e) -> p k e", k=8),
                        wt[hhalf * 1024:(hhalf + 1) * 1024, :]
                        .rearrange("(k p) e -> p k e", p=128),
                        writes=[wbB[wpar]] if hhalf == 0 else [wbB[wpar]])
                    def mmh(e, wpar=wpar, hhalf=hhalf):
                        for k in range(8):
                            hc = hhalf * 8 + k
                            inst = e.matmul(
                                pb[wpar][:],
                                wbuf[wpar][:, k * 128:(k + 1) * 128],
                                xnT[:, hc * 512:(hc + 1) * 512],
                                start=(hc == 0), stop=(hc == 15))
                        return inst
                    S.step("tensor", mmh,
                           reads=[wbB[wpar], xnTB], writes=[pbB[wpar]])
                if oc < 16:
                    proj_epilogue_qk(oc, pb[wpar], pbB[wpar], qT, oc * 512,
                                     qTB, cosb[par], sinb[par], cosB[par])
                elif oc < 20:
                    kvh = oc - 16
                    slot = par * 4 + kvh
                    proj_epilogue_qk(oc, pb[wpar], pbB[wpar], kT, slot * 512,
                                     kTB[slot], cosb[par], sinb[par],
                                     cosB[par])
                else:
                    kvh = oc - 20
                    slot = par * 4 + kvh
                    vf, vfB = scr[0], gb("scr0")
                    S.step("vector",
                           lambda e, wpar=wpar: e.tensor_copy(vf[:], pb[wpar][:]),
                           reads=[pbB[wpar]], writes=[vfB])
                    def vtr(e, vf=vf):
                        for k in range(4):
                            inst = e.transpose(pb[5][:, k * 128:(k + 1) * 128],
                                               vf[:, k * 128:(k + 1) * 128],
                                               ident[:])
                        return inst
                    S.step("tensor", vtr, reads=[vfB, gb("ident")],
                           writes=[pbB[5]])
                    S.step("vector",
                           lambda e, slot=slot: e.tensor_copy(
                               vC[:, slot * 512:(slot + 1) * 512], pb[5][:]),
                           reads=[pbB[5]], writes=[vCB[slot]])

            tap(f"qT_s{s}", qT[:], qTB)
            tap(f"kT_s{s}", kT[:], kTB)
            tap(f"vC_s{s}", vC[:], vCB)
            if s == 0:
                continue
            a = s - 1
            for half in range(2):
                for tih in range(2):
                    ti = 4 * a + 2 * half + tih
                    mpar = ti % 2
                    dma("sync", maskb[mpar][:].rearrange("p (k e) -> p k e", k=5),
                        d_mask[ti].transpose([1, 0, 2]), writes=[maskB[mpar]])
                    qcol = (ti - 4 * a) * 128

                    def emit_scores(h, par):
                        kvh = h // 4

                        def smm(e, ti=ti, h=h, kvh=kvh, qcol=qcol, par=par):
                            for kb in range(5):
                                ek = ti + kb
                                slot = ((ek // 4) % 2) * 4 + kvh
                                kap = kT[:, slot * 512 + (ek % 4) * 128:
                                         slot * 512 + (ek % 4) * 128 + 128]
                                if par == 0:
                                    dst = score_ps[:, kb * 128:(kb + 1) * 128]
                                elif kb < 4:
                                    dst = pb[0][:, kb * 128:(kb + 1) * 128]
                                else:
                                    dst = pb[1][:, 0:128]
                                inst = e.matmul(
                                    dst, kap,
                                    qT[:, h * 512 + qcol:h * 512 + qcol + 128],
                                    start=(kb in (0, 4)), stop=(kb in (3, 4)))
                            return inst
                        ek_slots = [((ti + kb) // 4) % 2 * 4 + kvh
                                    for kb in range(5)]
                        wb = [scoreB] if par == 0 else [pbB[0], pbB[1]]
                        S.step("tensor", smm,
                               reads=[qTB] + [kTB[sl] for sl in set(ek_slots)],
                               writes=wb)

                    def emit_probs(h, par):
                        pr, prB = (prE, prEB) if par == 0 else (prE2, prE2B)
                        if par == 0:
                            S.step("vector",
                                   lambda e, mpar=mpar, pr=pr:
                                   e.scalar_tensor_tensor(
                                       pr[:], score_ps[:], SCALE,
                                       maskb[mpar][:], ALU.mult, ALU.add),
                                   reads=[scoreB, maskB[mpar]], writes=[prB])
                        else:
                            S.step("vector",
                                   lambda e, mpar=mpar, pr=pr:
                                   e.scalar_tensor_tensor(
                                       pr[:, 0:512], pb[0][:], SCALE,
                                       maskb[mpar][:, 0:512],
                                       ALU.mult, ALU.add),
                                   reads=[pbB[0], maskB[mpar]], writes=[prB])
                            S.step("vector",
                                   lambda e, mpar=mpar, pr=pr:
                                   e.scalar_tensor_tensor(
                                       pr[:, 512:640], pb[1][:, 0:128], SCALE,
                                       maskb[mpar][:, 512:640],
                                       ALU.mult, ALU.add),
                                   reads=[pbB[1], maskB[mpar]], writes=[prB])
                        S.step("scalar",
                               lambda e, pr=pr: e.activation(pr[:], pr[:],
                                                             AF.Exp),
                               reads=[prB], writes=[prB])

                    def emit_avz(h, par):
                        kvh = h // 4
                        pr, prB = (prE, prEB) if par == 0 else (prE2, prE2B)
                        az, azB = (pb[4], pbB[4]) if par == 0 else (pb[5], pbB[5])

                        def avz(e, ti=ti, kvh=kvh, pr=pr, az=az):
                            for kb in range(5):
                                ek = ti + kb
                                slot = ((ek // 4) % 2) * 4 + kvh
                                vap = vC[:, slot * 512 + (ek % 4) * 128:
                                         slot * 512 + (ek % 4) * 128 + 128]
                                e.matmul(az[:, 0:128], vap,
                                         pr[:, kb * 128:(kb + 1) * 128],
                                         start=(kb == 0), stop=False)
                            for kb in range(5):
                                inst = e.matmul(az[:, 128:256], ones[:],
                                                pr[:, kb * 128:(kb + 1) * 128],
                                                start=False, stop=(kb == 4))
                            return inst
                        vslots = [((ti + kb) // 4) % 2 * 4 + kvh
                                  for kb in range(5)]
                        S.step("tensor", avz,
                               reads=[prB, gb("ones")] +
                                     [vCB[sl] for sl in set(vslots)],
                               writes=[azB])
                        tap(f"prE_t{ti}h{h}", pr[:], prB)
                        rzs, rzsB = (rz, rzB) if par == 0 else (rz2, rz2B)
                        S.step("vector",
                               lambda e, az=az, rzs=rzs: e.reciprocal(
                                   rzs[:], az[:, 128:256]),
                               reads=[azB], writes=[rzsB])
                        ccol = (tih * NHC + h) * 128
                        S.step("vector",
                               lambda e, ccol=ccol, az=az, rzs=rzs:
                               e.tensor_mul(concat[:, ccol:ccol + 128],
                                            az[:, 0:128], rzs[:]),
                               reads=[azB, rzsB], writes=[concatB])

                    emit_scores(0, 0)
                    for h in range(NHEAD):
                        emit_probs(h, h % 2)
                        if h + 1 < NHEAD:
                            emit_scores(h + 1, (h + 1) % 2)
                        emit_avz(h, h % 2)
                ti0 = 4 * a + 2 * half
                for tih in range(2):
                    ti = ti0 + tih
                    xs_sb, xs_buf = xring[ti % 3], xrB[ti % 3]
                    dma("sync", xs_sb[:],
                        d_xs[(ti + 4) * 128:(ti + 5) * 128, :],
                        writes=[xs_buf])
                for op_pair in range(2):
                    for hc in range(NHC):
                        wop = hc % 2
                        dma("sync", wob[wop][:],
                            d_wo[hc * 128:(hc + 1) * 128,
                                 op_pair * 1024:(op_pair + 1) * 1024],
                            writes=[woB[wop]])
                        def opmm(e, hc=hc, wop=wop):
                            for tih in range(2):
                                for occi in range(2):
                                    inst = e.matmul(
                                        pb[tih * 2 + occi][:],
                                        concat[:, (tih * NHC + hc) * 128:
                                               (tih * NHC + hc) * 128 + 128],
                                        wob[wop][:, occi * 512:
                                                 (occi + 1) * 512],
                                        start=(hc == 0), stop=(hc == NHC - 1))
                            return inst
                        S.step("tensor", opmm,
                               reads=[concatB, woB[wop]],
                               writes=[pbB[0], pbB[1], pbB[2], pbB[3]])
                    for tih in range(2):
                        ti = ti0 + tih
                        xs_sb, xs_buf = xring[ti % 3], xrB[ti % 3]
                        ost, ostB = pool2k[tih], p2kB[tih]
                        for occi in range(2):
                            col = (op_pair * 2 + occi) * 512
                            S.step("vector",
                                   lambda e, tih=tih, occi=occi, col=col,
                                          ost=ost, xs_sb=xs_sb: e.tensor_add(
                                       ost[:, col:col + 512],
                                       pb[tih * 2 + occi][:],
                                       xs_sb[:, col:col + 512]),
                                   reads=[pbB[tih * 2 + occi], xs_buf],
                                   writes=[ostB])
                for tih in range(2):
                    ti = ti0 + tih
                    dma("sync", d_out[ti * 128:(ti + 1) * 128, :],
                        pool2k[tih][:], reads=[p2kB[tih]],
                        writes=[gb("outrows")])



    if "B" in phases:
        S.barrier()
        idxB = gb("b_idx")
        dma("gpsimd", b_idx[:], d_didx.rearrange("(j p) -> p j", p=128),
            writes=[idxB], allow_slow_non_contiguous=True)
        bwB = [gb(f"b_w{i}") for i in range(2)]
        bw8B = [gb(f"b_w8{i}") for i in range(2)]
        bstB = [gb(f"b_st{i}") for i in range(2)]
        bmB = gb("b_m")
        wsgB, wsuB, ws2B = gb("b_wsg"), gb("b_wsu"), gb("b_ws2")
        scrB = {"g": gb("scr_g8"), "u": gb("scr_u8"), "d": gb("scr_d8")}

        for wname, d_src, d_scr, ws_sb, ws_buf in (
                ("g", d_wg, d_g8, b_wsg, wsgB), ("u", d_wu, d_u8, b_wsu, wsuB)):
            for j in range(NJ):
                par = j % 2
                mc = par * 2
                S.step("gpsimd",
                       lambda e, par=par, j=j, d_src=d_src:
                       e.indirect_dma_start(
                           out=b_w[par][:, 0:H], out_offset=None,
                           in_=d_src,
                           in_offset=IndirectOffsetOnAxis(b_idx[:, j:j + 1], 0)),
                       reads=[idxB], writes=[bwB[par]], dma=True)
                S.step("vector",
                       lambda e, par=par, mc=mc: e.reduce_max(
                           b_m[:, mc:mc + 1], b_w[par][:, 0:H], AX.X,
                           apply_absolute_value=True),
                       reads=[bwB[par]], writes=[bmB], narrow=True)
                S.step("vector",
                       lambda e, mc=mc: e.reciprocal(b_m[:, mc + 1:mc + 2],
                                                     b_m[:, mc:mc + 1]),
                       reads=[bmB], writes=[bmB], narrow=True)
                S.step("vector",
                       lambda e, mc=mc: e.tensor_scalar(
                           b_m[:, mc + 1:mc + 2], b_m[:, mc + 1:mc + 2],
                           224.0, None, ALU.mult),
                       reads=[bmB], writes=[bmB], narrow=True)
                S.step("vector",
                       lambda e, j=j, ws_sb=ws_sb, mc=mc: e.tensor_scalar(
                           ws_sb[:, j:j + 1], b_m[:, mc:mc + 1],
                           1.0 / 224.0, None, ALU.mult),
                       reads=[bmB], writes=[ws_buf], narrow=True)
                S.step("vector",
                       lambda e, par=par, mc=mc: e.tensor_scalar(
                           b_w8[par][:, 0:H], b_w[par][:, 0:H],
                           b_m[:, mc + 1:mc + 2], None, ALU.mult),
                       reads=[bwB[par], bmB], writes=[bw8B[par]])
                for grp in range(2):
                    bank = pb[4 + grp]
                    bankB = pbB[4 + grp]
                    def tr8(e, par=par, grp=grp):
                        bk8 = pb[4 + grp][:].bitcast(FP8)
                        for k in range(8):
                            c = grp * 8 + k
                            inst = e.transpose(
                                bk8[:, k * 256:(k + 1) * 256:2],
                                b_w8[par][:, c * 128:(c + 1) * 128],
                                ident8[:])
                        return inst
                    S.step("tensor", tr8,
                           reads=[bw8B[par], gb("ident8")], writes=[bankB])
                    def ev8(e, par=par, grp=grp):
                        src = pb[4 + grp][:].bitcast(FP8).rearrange(
                            "p (k s) -> p k s", k=8)[:, :, 0:256:2]
                        dst = b_stage[par][:, grp * 1024:(grp + 1) * 1024]\
                            .rearrange("p (k e) -> p k e", k=8)
                        return e.tensor_copy(dst, src)
                    S.step("vector", ev8, reads=[bankB], writes=[bstB[par]])
                dma("scalar", d_scr[j], b_stage[par][:, 0:H],
                    reads=[bstB[par]], writes=[scrB[wname]])
        for t16 in range(NHC):
            par = t16 % 2
            mc = par * 2
            dma("sync", b_w[par][:], d_wd[t16 * 128:(t16 + 1) * 128, :],
                writes=[bwB[par]])
            S.step("vector",
                   lambda e, par=par, mc=mc: e.reduce_max(
                       b_m[:, mc:mc + 1], b_w[par][:], AX.X,
                       apply_absolute_value=True),
                   reads=[bwB[par]], writes=[bmB], narrow=True)
            S.step("vector",
                   lambda e, mc=mc: e.reciprocal(b_m[:, mc + 1:mc + 2],
                                                 b_m[:, mc:mc + 1]),
                   reads=[bmB], writes=[bmB], narrow=True)
            S.step("vector",
                   lambda e, mc=mc: e.tensor_scalar(
                       b_m[:, mc + 1:mc + 2], b_m[:, mc + 1:mc + 2],
                       224.0, None, ALU.mult),
                   reads=[bmB], writes=[bmB], narrow=True)
            S.step("vector",
                   lambda e, t16=t16, mc=mc: e.tensor_scalar(
                       b_ws2[:, t16:t16 + 1], b_m[:, mc:mc + 1],
                       1.0 / 224.0, None, ALU.mult),
                   reads=[bmB], writes=[ws2B], narrow=True)
            S.step("vector",
                   lambda e, par=par, mc=mc: e.tensor_scalar(
                       b_w8[par][:], b_w[par][:],
                       b_m[:, mc + 1:mc + 2], None, ALU.mult),
                   reads=[bwB[par], bmB], writes=[bw8B[par]])
            for grp in range(6):
                npc = 8 if grp < 5 else 4
                bank = 4 + grp % 2
                def tr8d(e, par=par, grp=grp, npc=npc, bank=bank):
                    bk8 = pb[bank][:].bitcast(FP8)
                    for k in range(npc):
                        jc = grp * 8 + k
                        inst = e.transpose(
                            bk8[:, k * 256:(k + 1) * 256:2],
                            b_w8[par][:, jc * 128:(jc + 1) * 128],
                            ident8[:])
                    return inst
                S.step("tensor", tr8d,
                       reads=[bw8B[par], gb("ident8")], writes=[pbB[bank]])
                def ev8d(e, par=par, grp=grp, npc=npc, bank=bank):
                    src = pb[bank][:].bitcast(FP8).rearrange(
                        "p (k s) -> p k s", k=8)[:, 0:npc, 0:256:2]
                    dst = b_stage[par][:, grp * 1024:grp * 1024 + npc * 128]\
                        .rearrange("p (k e) -> p k e", k=npc)
                    return e.tensor_copy(dst, src)
                S.step("vector", ev8d, reads=[pbB[bank]], writes=[bstB[par]])
            dma("scalar", d_d8[t16], b_stage[par][:],
                reads=[bstB[par]], writes=[scrB["d"]])

    if "C" in phases:
        S.barrier()
        cxB = [gb(f"c_x{i}") for i in range(2)]
        cxnB = gb("c_xn")
        cstB = gb("c_stats")
        cq8B, cq8TB = gb("c_qx8"), gb("c_qx8T")
        cfxB, cfxbB = gb("c_fx"), gb("c_fxb")
        cwgB = [gb(f"c_wg8{i}") for i in range(2)]
        cwuB = [gb(f"c_wu8{i}") for i in range(2)]
        cwdB = [gb(f"c_wd8{i}") for i in range(2)]
        ctB, crmB, cqtB = gb("c_t"), gb("c_rmax"), gb("c_qt8")
        crmnB = gb("c_rmin")
        cgscB, cuscB, cgsiB = gb("c_gsc"), gb("c_usc"), gb("c_gsi")
        cqsbB, cftbB = gb("c_qsb"), gb("c_ftb")
        cy2B = [gb(f"c_y2{i}") for i in range(2)]
        costB = [gb(f"c_ost{i}") for i in range(2)]
        crowB, cmtB = gb("c_row"), gb("c_mrow")
        scrB2 = {"g": gb("scr_g8"), "u": gb("scr_u8"), "d": gb("scr_d8")}
        wsgB, wsuB, ws2B = gb("b_wsg"), gb("b_wsu"), gb("b_ws2")
        UTIL = pb[4]
        UTILB = pbB[4]
        UTIL2 = pb[5]
        UTIL2B = pbB[5]

        for cc in range(NCH):
            for bp in range(2):
                tb = cc * 2 + bp
                S_x, S_xB = c_x[bp], cxB[bp]
                dma("sync", S_x[:], d_xs[(tb + 4) * 128:(tb + 5) * 128, :],
                    writes=[S_xB])
                emit_ln_g(S_x, S_xB, c_xn, cxnB, c_ost[0], costB[0],
                          c_stats, cstB)
                S.step("vector",
                       lambda e: e.reduce_max(c_stats[:, 5:6], c_xn[:], AX.X,
                                              apply_absolute_value=True),
                       reads=[cxnB], writes=[cstB], narrow=True)
                S.step("vector",
                       lambda e: e.reciprocal(c_stats[:, 6:7], c_stats[:, 5:6]),
                       reads=[cstB], writes=[cstB], narrow=True)
                S.step("vector",
                       lambda e: e.tensor_scalar(c_stats[:, 6:7],
                                                 c_stats[:, 6:7], 224.0, None,
                                                 ALU.mult),
                       reads=[cstB], writes=[cstB], narrow=True)
                S.step("vector",
                       lambda e, bp=bp: e.tensor_scalar(
                           c_fx[:, bp:bp + 1], c_stats[:, 5:6],
                           1.0 / 224.0, None, ALU.mult),
                       reads=[cstB], writes=[cfxB], narrow=True)
                S.step("vector",
                       lambda e: e.tensor_scalar(c_qx8[:], c_xn[:],
                                                 c_stats[:, 6:7], None,
                                                 ALU.mult),
                       reads=[cxnB, cstB], writes=[cq8B])
                if cc == TAP_CC and bp == 0:
                    tap("c_xn0", c_xn[:], cxnB)
                    tap("c_qx80", c_qx8[:], cq8B)
                    tap("c_stats0", c_stats[:], cstB)
                for grp in range(2):
                    bank = (UTIL, UTIL2)[grp]
                    bankB = (UTILB, UTIL2B)[grp]
                    def trq(e, grp=grp, bank=bank):
                        bk8 = bank[:].bitcast(FP8)
                        for k in range(8):
                            hc = grp * 8 + k
                            inst = e.transpose(
                                bk8[:, k * 256:(k + 1) * 256:2],
                                c_qx8[:, hc * 128:(hc + 1) * 128], ident8[:])
                        return inst
                    S.step("tensor", trq, reads=[cq8B, gb("ident8")],
                           writes=[bankB])
                    def evq(e, grp=grp, bp=bp, bank=bank):
                        src = bank[:].bitcast(FP8).rearrange(
                            "p (k s) -> p k s", k=8)[:, :, 0:256:2]
                        dst = bass.AP(c_qx8T, (grp * 8) * CC + bp * 128,
                                      [[NHC * CC, 128], [CC, 8], [1, 128]])
                        return e.tensor_copy(dst, src)
                    S.step("vector", evq, reads=[bankB], writes=[cq8TB])
                S.step("tensor",
                       lambda e, bp=bp: e.matmul(
                           pb[0][0:1, bp * 128:(bp + 1) * 128],
                           c_fx[:, bp:bp + 1], ident[:],
                           start=(bp == 0), stop=(bp == 1)),
                       reads=[cfxB, gb("ident")], writes=[pbB[0]])
            S.step("vector",
                   lambda e: e.tensor_copy(c_row[0:1, 0:CC], pb[0][0:1, 0:CC]),
                   reads=[pbB[0]], writes=[crowB], narrow=True)
            S.step("tensor",
                   lambda e: e.matmul(UTIL2[:, 0:CC], ones[0:1, :],
                                      c_row[0:1, 0:CC], start=True, stop=True),
                   reads=[crowB, gb("ones")], writes=[UTIL2B])
            S.step("vector",
                   lambda e: e.tensor_copy(c_fxb[:], UTIL2[:, 0:CC]),
                   reads=[UTIL2B], writes=[cfxbB])
            if cc == TAP_CC:
                tap("c_qx8T0", c_qx8T[:], cq8TB)
                tap("c_fxb0", c_fxb[:], cfxbB)
            for j in range(NJ):
                par = j % 2
                pg, pgB = pb[2 * par], pbB[2 * par]
                pu, puB = pb[2 * par + 1], pbB[2 * par + 1]
                dma("sync", c_wg8[par][:], d_g8[j], writes=[cwgB[par]])
                dma("sync", c_wu8[par][:], d_u8[j], writes=[cwuB[par]])
                def gmm(e, par=par, pg=pg):
                    for hc in range(NHC):
                        inst = e.matmul(pg[:, 0:CC],
                                        c_wg8[par][:, hc * 128:(hc + 1) * 128],
                                        c_qx8T[:, hc * CC:(hc + 1) * CC],
                                        start=(hc == 0), stop=(hc == NHC - 1))
                    return inst
                S.step("tensor", gmm, reads=[cwgB[par], cq8TB], writes=[pgB])
                def umm(e, par=par, pu=pu):
                    for hc in range(NHC):
                        inst = e.matmul(pu[:, 0:CC],
                                        c_wu8[par][:, hc * 128:(hc + 1) * 128],
                                        c_qx8T[:, hc * CC:(hc + 1) * CC],
                                        start=(hc == 0), stop=(hc == NHC - 1))
                    return inst
                S.step("tensor", umm, reads=[cwuB[par], cq8TB], writes=[puB])
                S.step("vector",
                       lambda e, j=j, pg=pg: e.scalar_tensor_tensor(
                           c_gsc[:], pg[:, 0:CC], b_wsg[:, j:j + 1], c_fxb[:],
                           ALU.mult, ALU.mult),
                       reads=[pgB, wsgB, cfxbB], writes=[cgscB])
                S.step("scalar",
                       lambda e: e.activation(c_gsi[:], c_gsc[:], AF.Silu),
                       reads=[cgscB], writes=[cgsiB])
                S.step("vector",
                       lambda e, j=j, pu=pu: e.scalar_tensor_tensor(
                           c_usc[:], pu[:, 0:CC], b_wsu[:, j:j + 1], c_fxb[:],
                           ALU.mult, ALU.mult),
                       reads=[puB, wsuB, cfxbB], writes=[cuscB])
                if cc == TAP_CC and j == TAP_J:
                    tap("c_gsc0", c_gsc[:], cgscB)
                    tap("c_usc0", c_usc[:], cuscB)
                    tap("c_wg80", c_wg8[j % 2][:], cwgB[j % 2])
                S.step("gpsimd",
                       lambda e, j=j: e.tensor_mul(
                           c_t[:, j * CC:(j + 1) * CC], c_gsi[:], c_usc[:]),
                       reads=[cgsiB, cuscB], writes=[ctB])
                if j == 0:
                    S.step("vector",
                           lambda e: e.tensor_copy(c_rmax[:], c_t[:, 0:CC]),
                           reads=[ctB], writes=[crmB])
                    S.step("vector",
                           lambda e: e.tensor_copy(c_rmin[:], c_t[:, 0:CC]),
                           reads=[ctB], writes=[crmnB])
                else:
                    S.step("vector",
                           lambda e, j=j: e.tensor_max(
                               c_rmax[:], c_rmax[:],
                               c_t[:, j * CC:(j + 1) * CC]),
                           reads=[ctB, crmB], writes=[crmB])
                    S.step("vector",
                           lambda e, j=j: e.tensor_tensor(
                               c_rmin[:], c_rmin[:],
                               c_t[:, j * CC:(j + 1) * CC], ALU.min),
                           reads=[ctB, crmnB], writes=[crmnB])
            S.step("vector",
                   lambda e: e.tensor_scalar(c_rmin[:], c_rmin[:], -1.0,
                                             None, ALU.mult),
                   reads=[crmnB], writes=[crmnB])
            S.step("vector",
                   lambda e: e.tensor_max(c_rmax[:], c_rmax[:], c_rmin[:]),
                   reads=[crmnB, crmB], writes=[crmB])
            if cc == TAP_CC:
                tap("c_t0", c_t[:], ctB)
                tap("c_rmax0", c_rmax[:], crmB)
                tap("b_wsg0", b_wsg[:], wsgB)
                tap("b_ws20", b_ws2[:], ws2B)
            for bp in range(2):
                S.step("tensor",
                       lambda e, bp=bp: e.transpose(
                           UTIL[:, bp * 128:(bp + 1) * 128],
                           c_rmax[:, bp * 128:(bp + 1) * 128], ident[:]),
                       reads=[crmB, gb("ident")], writes=[UTILB])
                S.step("vector",
                       lambda e, bp=bp: e.reduce_max(
                           c_mrow[:, bp:bp + 1],
                           UTIL[:, bp * 128:(bp + 1) * 128], AX.X),
                       reads=[UTILB], writes=[cmtB], narrow=True)
                S.step("tensor",
                       lambda e, bp=bp: e.matmul(
                           UTIL2[0:1, bp * 128:(bp + 1) * 128],
                           c_mrow[:, bp:bp + 1], ident[:],
                           start=(bp == 0), stop=(bp == 1)),
                       reads=[cmtB, gb("ident")], writes=[UTIL2B])
            S.step("vector",
                   lambda e: e.tensor_copy(c_row[0:1, 256:256 + CC],
                                           UTIL2[0:1, 0:CC]),
                   reads=[UTIL2B], writes=[crowB], narrow=True)
            S.step("vector",
                   lambda e: e.reciprocal(c_row[0:1, 0:CC],
                                          c_row[0:1, 256:256 + CC]),
                   reads=[crowB], writes=[crowB], narrow=True)
            S.step("vector",
                   lambda e: e.tensor_scalar(c_row[0:1, 0:CC],
                                             c_row[0:1, 0:CC], 224.0, None,
                                             ALU.mult),
                   reads=[crowB], writes=[crowB], narrow=True)
            S.step("vector",
                   lambda e: e.tensor_scalar(c_row[0:1, 256:256 + CC],
                                             c_row[0:1, 256:256 + CC],
                                             1.0 / 224.0, None, ALU.mult),
                   reads=[crowB], writes=[crowB], narrow=True)
            S.step("tensor",
                   lambda e: e.matmul(UTIL[:, 0:CC], ones[0:1, :],
                                      c_row[0:1, 0:CC], start=True, stop=True),
                   reads=[crowB, gb("ones")], writes=[UTILB])
            S.step("vector", lambda e: e.tensor_copy(c_qsb[:], UTIL[:, 0:CC]),
                   reads=[UTILB], writes=[cqsbB])
            S.step("tensor",
                   lambda e: e.matmul(UTIL[:, 0:CC], ones[0:1, :],
                                      c_row[0:1, 256:256 + CC],
                                      start=True, stop=True),
                   reads=[crowB, gb("ones")], writes=[UTILB])
            S.step("vector", lambda e: e.tensor_copy(c_ftb[:], UTIL[:, 0:CC]),
                   reads=[UTILB], writes=[cftbB])
            if cc == TAP_CC:
                tap("c_qsb0", c_qsb[:], cqsbB)
                tap("c_ftb0", c_ftb[:], cftbB)
            for j in range(NJ):
                S.step("vector",
                       lambda e, j=j: e.tensor_tensor(
                           c_qt8[:, j * CC:(j + 1) * CC],
                           c_t[:, j * CC:(j + 1) * CC], c_qsb[:], ALU.mult),
                       reads=[ctB, cqsbB], writes=[cqtB])
            if cc == TAP_CC:
                tap("c_qt80", c_qt8[:], cqtB)
            for hc in range(NHC):
                par = hc % 2
                pd, pdB = pb[2 + par], pbB[2 + par]
                dma("sync", c_wd8[par][:], d_d8[hc], writes=[cwdB[par]])
                def dmm(e, par=par, pd=pd):
                    for jc in range(NJ):
                        inst = e.matmul(pd[:, 0:CC],
                                        c_wd8[par][:, jc * 128:(jc + 1) * 128],
                                        c_qt8[:, jc * CC:(jc + 1) * CC],
                                        start=(jc == 0), stop=(jc == NJ - 1))
                    return inst
                S.step("tensor", dmm, reads=[cwdB[par], cqtB], writes=[pdB])
                S.step("vector",
                       lambda e, hc=hc, par=par, pd=pd: e.scalar_tensor_tensor(
                           c_y2[par][:, 0:CC], pd[:, 0:CC],
                           b_ws2[:, hc:hc + 1], c_ftb[:], ALU.mult, ALU.mult),
                       reads=[pdB, ws2B, cftbB], writes=[cy2B[par]])
                for bp in range(2):
                    S.step("tensor",
                           lambda e, hc=hc, par=par, bp=bp: e.transpose(
                               UTIL[:, bp * 128:(bp + 1) * 128],
                               c_y2[par][:, bp * 128:(bp + 1) * 128],
                               ident[:]),
                           reads=[cy2B[par], gb("ident")], writes=[UTILB])
                    S.step("vector",
                           lambda e, hc=hc, bp=bp: e.tensor_copy(
                               c_ost[bp][:, hc * 128:(hc + 1) * 128],
                               UTIL[:, bp * 128:(bp + 1) * 128]),
                           reads=[UTILB], writes=[costB[bp]])
            for bp in range(2):
                tb = cc * 2 + bp
                dma("gpsimd", d_out[tb * 128:(tb + 1) * 128, :],
                    c_ost[bp][:], accum_op=ALU.add,
                    reads=[costB[bp]], writes=[gb("outrows")])

    S.barrier()
    final_waits = [(n, c) for n, c in S.cnt.items() if n.startswith("q_")]
    S.step("gpsimd", lambda e: e.nop(), extra_waits=final_waits)

    sems = {}
    from contextlib import ExitStack
    with ExitStack() as stack:
        for name in S.cnt:
            sems[name] = stack.enter_context(nc.semaphore(name))
        S.finalize(nc, sems)
    return nc


def assemble(results):
    out = np.empty((BATCH, T, H), dtype=np.float32)
    for c in range(NCORE):
        b, hh = c // 2, c % 2
        out[b, hh * TOK:(hh + 1) * TOK] = results[c]["out"]
    return out


def kernel(**inputs):
    in_maps = stage_inputs(**inputs)
    nc = build_program()
    res = run_bass_kernel_spmd(nc, in_maps, core_ids=list(range(NCORE)))
    return assemble(res.results)


# version 03
# speedup vs baseline: 1.5431x; 1.0518x over previous
import numpy as np
import concourse.bass as bass
import concourse.mybir as mybir
from concourse.bass_utils import run_bass_kernel_spmd
from concourse.bass_core_rust import IndirectOffsetOnAxis

F32 = mybir.dt.float32
BF16 = mybir.dt.bfloat16
FP8 = mybir.dt.float8e4
U32 = mybir.dt.uint32
AF = mybir.ActivationFunctionType
ALU = mybir.AluOpType
AX = mybir.AxisListType

BATCH, T, H = 4, 4096, 2048
NCORE = 8
TOK = 2048
HALO = 512
TE = TOK + HALO
NB = TE // 128
NQ = TOK // 128
NSLAB = NB // 4
D, NHEAD, NKVH = 128, 16, 4
NHC = H // 128
II = 5632
NJ = II // 128
CC = 256
NCH = TOK // CC
SCALE = float(1.0 / np.sqrt(128.0))
NEGINF = -1.0e30
USE_DR_GU = False
USE_DR_DN = True
LN_EPS = 1e-5
TAP_CC = 0
TAP_J = 9

ENGINES = ("sync", "gpsimd", "tensor", "vector", "scalar")


class Buf:
    __slots__ = ("ready", "reads")

    def __init__(self):
        self.ready = None
        self.reads = []


class Sched:

    def __init__(self):
        self.prog = {e: [] for e in ENGINES}
        self.cnt = {}
        self.hwm = {e: {} for e in ENGINES}

    def step(self, eng, fn, reads=(), writes=(), sem=None, dma=False,
             extra_waits=(), narrow=False):
        if sem is None:
            sem = ("q_" + eng) if dma else ("c_" + eng)
        amt = 16 if dma else 1
        waits = {}

        def addw(dep):
            peng, psem, pcnt, upg = dep
            if peng != eng:
                if upg:
                    pcnt = self.cnt[psem]
                waits[psem] = max(waits.get(psem, 0), pcnt)

        for b in reads:
            if b.ready is not None:
                addw(b.ready)
        for b in writes:
            if b.ready is not None:
                addw(b.ready)
            for dep in b.reads:
                addw(dep)
        for psem, pcnt in extra_waits:
            waits[psem] = max(waits.get(psem, 0), pcnt)
        hwm = self.hwm[eng]
        wl = []
        for n in sorted(waits):
            v = waits[n]
            if v > hwm.get(n, 0):
                wl.append((n, v))
                hwm[n] = v
        self.cnt[sem] = self.cnt.get(sem, 0) + amt
        upg = dma and eng == "gpsimd"
        me = ((None if dma else eng), sem, self.cnt[sem], upg)
        for b in writes:
            b.ready = me
            b.reads = []
        for b in reads:
            b.reads.append(me)
        if narrow:
            inner = fn

            def fn(e, _inner=inner):
                _inner(e)
                return e.tensor_copy(self.junk[1][:], self.junk[0][:])
        self.prog[eng].append((wl, fn, sem, amt))
        return me

    def barrier(self):
        snap = [(n, c) for n, c in self.cnt.items()]
        for eng in ENGINES:
            self.step(eng, lambda e: e.nop(), extra_waits=snap)

    def finalize(self, nc, sems):
        with nc.Block() as blk:
            for eng_name in ENGINES:
                def body(e, _eng=eng_name):
                    for wl, fn, sem, amt in self.prog[_eng]:
                        for n, v in wl:
                            e.wait_ge(sems[n], v)
                        inst = fn(e)
                        if inst is None:
                            inst = e.nop()
                        inst.then_inc(sems[sem], amt)
                getattr(blk, eng_name)(body)



def _rope_tables():
    inv = 1.0 / (10000.0 ** (np.arange(0, D, 2, dtype=np.float64) / D))
    tabs = []
    for hh in (0, 1):
        pos = np.arange(hh * TOK - HALO, hh * TOK + TOK, dtype=np.float64)
        fr = pos[:, None] * inv[None, :]
        cos = np.concatenate([np.cos(fr), np.cos(fr)], axis=1)
        sin = np.concatenate([np.sin(fr), np.sin(fr)], axis=1)
        tabs.append((np.ascontiguousarray(cos.T.astype(np.float32)),
                     np.ascontiguousarray(sin.T.astype(np.float32))))
    return tabs


def _masks():
    out = []
    for hh in (0, 1):
        m = np.full((NQ, 5, 128, 128), NEGINF, dtype=np.float32)
        for ti in range(NQ):
            q = hh * TOK + ti * 128 + np.arange(128)
            for kb in range(5):
                k = hh * TOK - HALO + (ti + kb) * 128 + np.arange(128)
                dif = q[None, :] - k[:, None]
                ok = (dif >= 0) & (dif < HALO) & (k[:, None] >= 0)
                m[ti, kb][ok] = 0.0
        out.append(m)
    return out


def _rot_lhsT():
    M = np.zeros((128, 128), dtype=np.float32)
    for d in range(64):
        M[d, d + 64] = -1.0
        M[d + 64, d] = 1.0
    return np.ascontiguousarray(M.T)


def stage_inputs(L0_attn_x, L0_attn_wq, L0_attn_wk, L0_attn_wv, L0_attn_wo,
                 L0_ffn_w_gate, L0_ffn_w_up, L0_ffn_w_down,
                 L0_ffn_up_reorder_index, L0_ffn_down_reorder_index):
    x = np.asarray(L0_attn_x, dtype=np.float32)
    wq = np.ascontiguousarray(np.asarray(L0_attn_wq, np.float32)
                              .reshape(H, NHEAD, 128).transpose(1, 0, 2))
    wk = np.ascontiguousarray(np.asarray(L0_attn_wk, np.float32)
                              .reshape(H, NKVH, 128).transpose(1, 0, 2))
    wv = np.ascontiguousarray(np.asarray(L0_attn_wv, np.float32)
                              .reshape(H, NKVH, 128).transpose(1, 0, 2))
    wo = np.ascontiguousarray(np.asarray(L0_attn_wo, np.float32))
    wg = np.ascontiguousarray(np.asarray(L0_ffn_w_gate, np.float32))
    wu = np.ascontiguousarray(np.asarray(L0_ffn_w_up, np.float32))
    wd = np.ascontiguousarray(np.asarray(L0_ffn_w_down, np.float32))
    didx = np.ascontiguousarray(
        np.asarray(L0_ffn_down_reorder_index).astype(np.uint32))
    rope = _rope_tables()
    masks = _masks()
    rmat = _rot_lhsT()
    ident = np.ascontiguousarray(np.eye(128, dtype=np.float32))
    ones = np.ones((128, 128), dtype=np.float32)
    consts = np.zeros((128, 2), dtype=np.float32)
    consts[:, 0] = LN_EPS
    consts[:, 1] = 1.0 / H
    in_maps = []
    for c in range(NCORE):
        b, hh = c // 2, c % 2
        xs = np.zeros((TE, H), dtype=np.float32)
        lo = hh * TOK - HALO
        src_lo = max(lo, 0)
        xs[src_lo - lo:] = x[b, src_lo:hh * TOK + TOK]
        cosT, sinT = rope[hh]
        in_maps.append({
            "xs": np.ascontiguousarray(xs),
            "wq": wq, "wk": wk, "wv": wv, "wo": wo,
            "wg": wg, "wu": wu, "wd": wd, "didx": didx,
            "rcosT": cosT, "rsinT": sinT, "maskT": masks[hh],
            "rmat": rmat, "ident": ident, "ones": ones, "consts": consts,
        })
    return in_maps



def build_program(phases=("A", "B", "C"), taps=()):
    nc = bass.Bass("TRN2", target_bir_lowering=False, debug=False)
    S = Sched()

    d_xs = nc.dram_tensor("xs", [TE, H], F32, kind="ExternalInput").ap()
    d_wq = nc.dram_tensor("wq", [NHEAD, H, 128], F32, kind="ExternalInput").ap()
    d_wk = nc.dram_tensor("wk", [NKVH, H, 128], F32, kind="ExternalInput").ap()
    d_wv = nc.dram_tensor("wv", [NKVH, H, 128], F32, kind="ExternalInput").ap()
    d_wo = nc.dram_tensor("wo", [H, H], F32, kind="ExternalInput").ap()
    d_wg = nc.dram_tensor("wg", [II, H], F32, kind="ExternalInput").ap()
    d_wu = nc.dram_tensor("wu", [II, H], F32, kind="ExternalInput").ap()
    d_wd = nc.dram_tensor("wd", [H, II], F32, kind="ExternalInput").ap()
    d_didx = nc.dram_tensor("didx", [II], U32, kind="ExternalInput").ap()
    d_cos = nc.dram_tensor("rcosT", [128, TE], F32, kind="ExternalInput").ap()
    d_sin = nc.dram_tensor("rsinT", [128, TE], F32, kind="ExternalInput").ap()
    d_mask = nc.dram_tensor("maskT", [NQ, 5, 128, 128], F32,
                            kind="ExternalInput").ap()
    d_rmat = nc.dram_tensor("rmat", [128, 128], F32, kind="ExternalInput").ap()
    d_ident = nc.dram_tensor("ident", [128, 128], F32, kind="ExternalInput").ap()
    d_ones = nc.dram_tensor("ones", [128, 128], F32, kind="ExternalInput").ap()
    d_consts = nc.dram_tensor("consts", [128, 2], F32, kind="ExternalInput").ap()
    d_out = nc.dram_tensor("out", [TOK, H], F32, kind="ExternalOutput").ap()
    d_g8 = nc.dram_tensor("scr_g8", [NJ, 128, H], FP8, kind="Internal").ap()
    d_u8 = nc.dram_tensor("scr_u8", [NJ, 128, H], FP8, kind="Internal").ap()
    d_d8 = nc.dram_tensor("scr_d8", [NHC, 128, II], FP8, kind="Internal").ap()
    taps = set(taps)

    def tap(name, sb_ap, buf, eng="gpsimd"):
        if name not in taps:
            return
        shape = list(sb_ap.shape)
        d = nc.dram_tensor("tap_" + name, shape, sb_ap.dtype,
                           kind="ExternalOutput").ap()
        bl = buf if isinstance(buf, (list, tuple)) else [buf]
        dma(eng, d, sb_ap, reads=bl, writes=[gb("tapw")])

    sb_base = (nc._sbuf_addr_for_side("left") + 31) & ~31
    sb_limit = sb_base + nc.sbuf_bytes_remaining - 64
    off = [sb_base]
    sb = {}

    def alloc(name, shape, dtype, at=None):
        per = int(np.prod(shape[1:])) * mybir.dt.size(dtype)
        if at is None:
            at = off[0]
            off[0] = (at + per + 31) & ~31
        h = nc.alloc_sbuf_tensor_at(name, shape, dtype, offset=at)
        sb[name] = h
        return h

    consts = alloc("c_consts", [128, 2], F32)
    ident = alloc("c_ident", [128, 128], F32)
    ident8 = alloc("c_ident8", [128, 128], FP8)
    rmat = alloc("c_rmat", [128, 128], F32)
    ones = alloc("c_ones", [128, 128], F32)
    junk0 = alloc("c_junk0", [128, 128], F32)
    junk1 = alloc("c_junk1", [128, 128], F32)
    S.junk = (junk0, junk1)
    common_end = off[0]

    off[0] = common_end
    cosb = [alloc(f"a_cos{i}", [128, 512], F32) for i in range(2)]
    sinb = [alloc(f"a_sin{i}", [128, 512], F32) for i in range(2)]
    xring = [alloc(f"a_x{i}", [128, H], F32) for i in range(3)]
    pool2k = [alloc(f"a_p2k{i}", [128, H], F32) for i in range(3)]
    xnT = alloc("a_xnT", [128, NHC * 512], F32)
    wbuf = [alloc(f"a_wb{i}", [128, 8 * 128], F32) for i in range(2)]
    qT = alloc("a_qT", [128, NHEAD * 512], F32)
    kT = alloc("a_kT", [128, 2 * NKVH * 512], F32)
    vC = alloc("a_vC", [128, 2 * NKVH * 512], F32)
    scr = [alloc(f"a_scr{i}", [128, 512], F32) for i in range(5)]
    maskb = [alloc(f"a_mb{i}", [128, 640], F32) for i in range(2)]
    prE = alloc("a_prE", [128, 640], F32)
    prE2 = alloc("a_prE2", [128, 640], F32)
    rz = alloc("a_rz", [128, 128], F32)
    rz2 = alloc("a_rz2", [128, 128], F32)
    concat = alloc("a_concat", [128, 2 * NHC * 128], F32)
    wob = [alloc(f"a_wo{i}", [128, 1024], F32) for i in range(2)]
    a_end = off[0]
    assert a_end <= sb_limit, f"phase A SBUF over: {a_end - sb_base}"

    off[0] = common_end
    b_w = [alloc(f"b_w{i}", [128, II], F32) for i in range(2)]
    b_w8 = [alloc(f"b_w8{i}", [128, II], FP8) for i in range(2)]
    b_stage = [alloc(f"b_st{i}", [128, II], FP8) for i in range(2)]
    b_idx = alloc("b_idx", [128, NJ], U32)
    b_wsg = alloc("b_wsg", [128, NJ], F32)
    b_wsu = alloc("b_wsu", [128, NJ], F32)
    b_ws2 = alloc("b_ws2", [128, NHC], F32)
    b_m = alloc("b_m", [128, 4], F32)
    b_end = off[0]
    assert b_end <= sb_limit, f"phase B SBUF over: {b_end - sb_base}"

    b_keep_end = b_end
    off[0] = b_keep_end
    c_x = [alloc(f"c_x{i}", [128, H], F32) for i in range(2)]
    c_xn = alloc("c_xn", [128, H], F32)
    c_qx8 = alloc("c_qx8", [128, H], FP8)
    c_qx8T = alloc("c_qx8T", [128, NHC * CC], FP8)
    c_fx = alloc("c_fx", [128, 2], F32)
    c_fxb = alloc("c_fxb", [128, CC], F32)
    c_wg8 = [alloc(f"c_wg8{i}", [128, H], FP8) for i in range(2)]
    c_wu8 = [alloc(f"c_wu8{i}", [128, H], FP8) for i in range(2)]
    c_t = alloc("c_t", [128, NJ * CC], F32)
    c_rmax = alloc("c_rmax", [128, CC], F32)
    c_rmin = alloc("c_rmin", [128, CC], F32)
    c_qt8 = alloc("c_qt8", [128, NJ * CC], FP8)
    c_wd8 = [alloc(f"c_wd8{i}", [128, II], FP8) for i in range(2)]
    c_qsb = alloc("c_qsb", [128, CC], F32)
    c_ftb = alloc("c_ftb", [128, CC], F32)
    c_y2 = [alloc(f"c_y2{i}", [128, 512], F32) for i in range(2)]
    c_ost = [alloc(f"c_ost{i}", [128, H], F32) for i in range(2)]
    c_mrow = alloc("c_mrow", [128, 8], F32)
    c_stats = alloc("c_stats", [128, 8], F32)
    c_gsc = alloc("c_gsc", [128, CC], F32)
    c_usc = alloc("c_usc", [128, CC], F32)
    c_gsi = alloc("c_gsi", [128, CC], F32)
    c_row = alloc("c_row", [128, 512], F32)
    c_end = off[0]
    assert c_end <= sb_limit, f"phase C SBUF over: {c_end - sb_base}"

    pb = [nc.alloc_psum_tensor(f"pb{i}", [128, 512], F32) for i in range(6)]
    score_ps = nc.alloc_psum_tensor("score_ps", [128, 640], F32)

    bufs = {}

    def gb(name):
        if name not in bufs:
            bufs[name] = Buf()
        return bufs[name]

    pbB = [gb(f"pb{i}") for i in range(6)]
    scoreB = gb("score")

    def dma(eng, out_ap, in_ap, reads=(), writes=(), **kw):
        return S.step(eng, lambda e: e.dma_start(out=out_ap, in_=in_ap, **kw),
                      reads=reads, writes=writes, dma=True)

    cB = gb("consts")
    dma("sync", consts[:], d_consts, writes=[cB])
    dma("sync", ident[:], d_ident, writes=[gb("ident")])
    dma("sync", rmat[:], d_rmat, writes=[gb("rmat")])
    dma("sync", ones[:], d_ones, writes=[gb("ones")])
    S.step("vector", lambda e: e.tensor_copy(ident8[:], ident[:]),
           reads=[gb("ident")], writes=[gb("ident8")])

    EPS = consts[:, 0:1]
    INVH = consts[:, 1:2]

    def emit_ln_g(xsb, xbuf, xnsb, xnbuf, lnscr, lnscrB, stats, statsB):
        S.step("scalar",
               lambda e: e.activation(lnscr[:], xsb[:], AF.Copy,
                                      accum_out=stats[:, 0:1]),
               reads=[xbuf], writes=[lnscrB, statsB])
        S.step("vector",
               lambda e: e.tensor_scalar(stats[:, 1:2], stats[:, 0:1],
                                         -1.0 / H, None, ALU.mult),
               reads=[statsB], writes=[statsB])
        S.step("scalar",
               lambda e: e.activation(xnsb[:], xsb[:], AF.Identity,
                                      bias=stats[:, 1:2]),
               reads=[xbuf, statsB], writes=[xnbuf])
        S.step("vector",
               lambda e: e.scalar_tensor_tensor(
                   lnscr[:], xnsb[:], 1.0, xnsb[:], ALU.bypass, ALU.mult,
                   accum_out=stats[:, 2:3]),
               reads=[xnbuf], writes=[lnscrB, statsB])
        S.step("scalar",
               lambda e: e.activation(stats[:, 3:4], stats[:, 2:3], AF.Sqrt,
                                      bias=EPS, scale=INVH),
               reads=[statsB, cB], writes=[statsB])
        S.step("vector",
               lambda e: e.reciprocal(stats[:, 4:5], stats[:, 3:4]),
               reads=[statsB], writes=[statsB])
        S.step("scalar",
               lambda e: e.activation(xnsb[:], xnsb[:], AF.Copy,
                                      scale=stats[:, 4:5]),
               reads=[statsB, xnbuf], writes=[xnbuf])

    if "A" in phases:
        lnscr = pool2k[2]
        lnscrB = gb("lnscr")
        stats = scr[4]
        statsB = gb("stats")

        def emit_ln(xsb, xbuf, xnsb, xnbuf):
            emit_ln_g(xsb, xbuf, xnsb, xnbuf, lnscr, lnscrB, stats, statsB)

        xnTB = gb("xnT")
        qTB = gb("qT")
        kTB = [gb(f"kT{i}") for i in range(8)]
        vCB = [gb(f"vC{i}") for i in range(8)]
        cosB = [gb(f"cos{i}") for i in range(2)]
        maskB = [gb(f"mask{i}") for i in range(2)]
        concatB = gb("concat")
        prEB = gb("prE")
        prE2B = gb("prE2")
        rzB = gb("rz")
        rz2B = gb("rz2")
        woB = [gb(f"wo{i}") for i in range(2)]
        wbB = [gb(f"wb{i}") for i in range(2)]
        xrB = [gb(f"xr{i}") for i in range(3)]
        p2kB = [gb(f"p2k{i}") for i in range(3)]

        def proj_epilogue_qk(oc, bank, bank_buf, dst_sb, dst_col, dst_buf,
                             cos_sb, sin_sb, cs_buf):
            qsb, qsbB = scr[0], gb("scr0")
            sq, sqB = scr[1], gb("scr1")
            r_, rB = scr[2], gb("scr2")
            a_, aB = scr[3], gb("scr3")
            S.step("scalar", lambda e: e.activation(qsb[:], bank[:], AF.Copy),
                   reads=[bank_buf], writes=[qsbB])
            S.step("scalar", lambda e: e.activation(sq[:], bank[:], AF.Square),
                   reads=[bank_buf], writes=[sqB])
            S.step("tensor",
                   lambda e: e.matmul(pb[2][:], ones[:], sq[:],
                                      start=True, stop=True),
                   reads=[sqB, gb("ones")], writes=[pbB[2]])
            S.step("tensor",
                   lambda e: e.matmul(pb[3][:], rmat[:], qsb[:],
                                      start=True, stop=True),
                   reads=[qsbB, gb("rmat")], writes=[pbB[3]])
            S.step("scalar", lambda e: e.activation(r_[:], pb[2][:], AF.Sqrt),
                   reads=[pbB[2]], writes=[rB])
            S.step("vector",
                   lambda e: e.tensor_scalar(r_[:], r_[:], 1e-12, None,
                                             ALU.max),
                   reads=[rB], writes=[rB])
            S.step("vector", lambda e: e.reciprocal(r_[:], r_[:]),
                   reads=[rB], writes=[rB])
            S.step("gpsimd", lambda e: e.tensor_mul(a_[:], qsb[:], cos_sb[:]),
                   reads=[qsbB, cs_buf], writes=[aB])
            S.step("vector",
                   lambda e: e.scalar_tensor_tensor(
                       sq[:], pb[3][:], 1.0, sin_sb[:], ALU.bypass, ALU.mult),
                   reads=[pbB[3], cs_buf], writes=[sqB])
            S.step("vector", lambda e: e.tensor_add(a_[:], a_[:], sq[:]),
                   reads=[sqB, aB], writes=[aB])
            S.step("vector",
                   lambda e: e.tensor_mul(dst_sb[:, dst_col:dst_col + 512],
                                          a_[:], r_[:]),
                   reads=[aB, rB], writes=[dst_buf])

        for s in range(NSLAB):
            par = s % 2
            dma("sync", cosb[par][:], d_cos[:, s * 512:(s + 1) * 512],
                writes=[cosB[par]])
            dma("sync", sinb[par][:], d_sin[:, s * 512:(s + 1) * 512],
                writes=[gb(f"sin{par}")])
            for bpos in range(4):
                eb = 4 * s + bpos
                xs_sb, xs_buf = xring[eb % 3], xrB[eb % 3]
                dma("sync", xs_sb[:], d_xs[eb * 128:(eb + 1) * 128, :],
                    writes=[xs_buf])
                xn_sb, xn_buf = pool2k[eb % 2], p2kB[eb % 2]
                emit_ln(xs_sb, xs_buf, xn_sb, xn_buf)
                tap(f"xn_e{eb}", xn_sb[:], xn_buf)
                for g in range(4):
                    bk, bkB = pb[4], pbB[4]
                    def tr_group(e, g=g, xn_sb=xn_sb):
                        for k in range(4):
                            hc = g * 4 + k
                            inst = e.transpose(
                                bk[:, k * 128:(k + 1) * 128],
                                xn_sb[:, hc * 128:(hc + 1) * 128], ident[:])
                        return inst
                    S.step("tensor", tr_group,
                           reads=[xn_buf, gb("ident")], writes=[bkB])
                    def evac(e, g=g, bpos=bpos):
                        src = bk[:].rearrange("p (k e) -> p k e", k=4)
                        dst = bass.AP(xnT, g * 4 * 512 + bpos * 128,
                                      [[NHC * 512, 128], [512, 4], [1, 128]])
                        return e.tensor_copy(dst, src)
                    S.step("vector", evac, reads=[bkB], writes=[xnTB])

            tap(f"xnT_s{s}", xnT[:], xnTB)
            for oc in range(24):
                wt = (d_wq[oc] if oc < 16 else
                      d_wk[oc - 16] if oc < 20 else d_wv[oc - 20])
                wpar = oc % 2
                for hhalf in range(2):
                    dma("sync", wbuf[wpar][:].rearrange("p (k e) -> p k e", k=8),
                        wt[hhalf * 1024:(hhalf + 1) * 1024, :]
                        .rearrange("(k p) e -> p k e", p=128),
                        writes=[wbB[wpar]] if hhalf == 0 else [wbB[wpar]])
                    def mmh(e, wpar=wpar, hhalf=hhalf):
                        for k in range(8):
                            hc = hhalf * 8 + k
                            inst = e.matmul(
                                pb[wpar][:],
                                wbuf[wpar][:, k * 128:(k + 1) * 128],
                                xnT[:, hc * 512:(hc + 1) * 512],
                                start=(hc == 0), stop=(hc == 15))
                        return inst
                    S.step("tensor", mmh,
                           reads=[wbB[wpar], xnTB], writes=[pbB[wpar]])
                if oc < 16:
                    proj_epilogue_qk(oc, pb[wpar], pbB[wpar], qT, oc * 512,
                                     qTB, cosb[par], sinb[par], cosB[par])
                elif oc < 20:
                    kvh = oc - 16
                    slot = par * 4 + kvh
                    proj_epilogue_qk(oc, pb[wpar], pbB[wpar], kT, slot * 512,
                                     kTB[slot], cosb[par], sinb[par],
                                     cosB[par])
                else:
                    kvh = oc - 20
                    slot = par * 4 + kvh
                    vf, vfB = scr[0], gb("scr0")
                    S.step("vector",
                           lambda e, wpar=wpar: e.tensor_copy(vf[:], pb[wpar][:]),
                           reads=[pbB[wpar]], writes=[vfB])
                    def vtr(e, vf=vf):
                        for k in range(4):
                            inst = e.transpose(pb[5][:, k * 128:(k + 1) * 128],
                                               vf[:, k * 128:(k + 1) * 128],
                                               ident[:])
                        return inst
                    S.step("tensor", vtr, reads=[vfB, gb("ident")],
                           writes=[pbB[5]])
                    S.step("vector",
                           lambda e, slot=slot: e.tensor_copy(
                               vC[:, slot * 512:(slot + 1) * 512], pb[5][:]),
                           reads=[pbB[5]], writes=[vCB[slot]])

            tap(f"qT_s{s}", qT[:], qTB)
            tap(f"kT_s{s}", kT[:], kTB)
            tap(f"vC_s{s}", vC[:], vCB)
            if s == 0:
                continue
            a = s - 1
            for half in range(2):
                for tih in range(2):
                    ti = 4 * a + 2 * half + tih
                    mpar = ti % 2
                    dma("sync", maskb[mpar][:].rearrange("p (k e) -> p k e", k=5),
                        d_mask[ti].transpose([1, 0, 2]), writes=[maskB[mpar]])
                    qcol = (ti - 4 * a) * 128

                    def emit_scores(h, par):
                        kvh = h // 4

                        def smm(e, ti=ti, h=h, kvh=kvh, qcol=qcol, par=par):
                            for kb in range(5):
                                ek = ti + kb
                                slot = ((ek // 4) % 2) * 4 + kvh
                                kap = kT[:, slot * 512 + (ek % 4) * 128:
                                         slot * 512 + (ek % 4) * 128 + 128]
                                if par == 0:
                                    dst = score_ps[:, kb * 128:(kb + 1) * 128]
                                elif kb < 4:
                                    dst = pb[0][:, kb * 128:(kb + 1) * 128]
                                else:
                                    dst = pb[1][:, 0:128]
                                inst = e.matmul(
                                    dst, kap,
                                    qT[:, h * 512 + qcol:h * 512 + qcol + 128],
                                    start=(kb in (0, 4)), stop=(kb in (3, 4)))
                            return inst
                        ek_slots = [((ti + kb) // 4) % 2 * 4 + kvh
                                    for kb in range(5)]
                        wb = [scoreB] if par == 0 else [pbB[0], pbB[1]]
                        S.step("tensor", smm,
                               reads=[qTB] + [kTB[sl] for sl in set(ek_slots)],
                               writes=wb)

                    def emit_probs(h, par):
                        pr, prB = (prE, prEB) if par == 0 else (prE2, prE2B)
                        if par == 0:
                            S.step("vector",
                                   lambda e, mpar=mpar, pr=pr:
                                   e.scalar_tensor_tensor(
                                       pr[:], score_ps[:], SCALE,
                                       maskb[mpar][:], ALU.mult, ALU.add),
                                   reads=[scoreB, maskB[mpar]], writes=[prB])
                        else:
                            S.step("vector",
                                   lambda e, mpar=mpar, pr=pr:
                                   e.scalar_tensor_tensor(
                                       pr[:, 0:512], pb[0][:], SCALE,
                                       maskb[mpar][:, 0:512],
                                       ALU.mult, ALU.add),
                                   reads=[pbB[0], maskB[mpar]], writes=[prB])
                            S.step("vector",
                                   lambda e, mpar=mpar, pr=pr:
                                   e.scalar_tensor_tensor(
                                       pr[:, 512:640], pb[1][:, 0:128], SCALE,
                                       maskb[mpar][:, 512:640],
                                       ALU.mult, ALU.add),
                                   reads=[pbB[1], maskB[mpar]], writes=[prB])
                        S.step("scalar",
                               lambda e, pr=pr: e.activation(pr[:], pr[:],
                                                             AF.Exp),
                               reads=[prB], writes=[prB])

                    def emit_avz(h, par):
                        kvh = h // 4
                        pr, prB = (prE, prEB) if par == 0 else (prE2, prE2B)
                        az, azB = (pb[4], pbB[4]) if par == 0 else (pb[5], pbB[5])

                        def avz(e, ti=ti, kvh=kvh, pr=pr, az=az):
                            for kb in range(5):
                                ek = ti + kb
                                slot = ((ek // 4) % 2) * 4 + kvh
                                vap = vC[:, slot * 512 + (ek % 4) * 128:
                                         slot * 512 + (ek % 4) * 128 + 128]
                                e.matmul(az[:, 0:128], vap,
                                         pr[:, kb * 128:(kb + 1) * 128],
                                         start=(kb == 0), stop=False)
                            for kb in range(5):
                                inst = e.matmul(az[:, 128:256], ones[:],
                                                pr[:, kb * 128:(kb + 1) * 128],
                                                start=False, stop=(kb == 4))
                            return inst
                        vslots = [((ti + kb) // 4) % 2 * 4 + kvh
                                  for kb in range(5)]
                        S.step("tensor", avz,
                               reads=[prB, gb("ones")] +
                                     [vCB[sl] for sl in set(vslots)],
                               writes=[azB])
                        tap(f"prE_t{ti}h{h}", pr[:], prB)
                        rzs, rzsB = (rz, rzB) if par == 0 else (rz2, rz2B)
                        S.step("vector",
                               lambda e, az=az, rzs=rzs: e.reciprocal(
                                   rzs[:], az[:, 128:256]),
                               reads=[azB], writes=[rzsB])
                        ccol = (tih * NHC + h) * 128
                        S.step("vector",
                               lambda e, ccol=ccol, az=az, rzs=rzs:
                               e.tensor_mul(concat[:, ccol:ccol + 128],
                                            az[:, 0:128], rzs[:]),
                               reads=[azB, rzsB], writes=[concatB])

                    emit_scores(0, 0)
                    for h in range(NHEAD):
                        emit_probs(h, h % 2)
                        if h + 1 < NHEAD:
                            emit_scores(h + 1, (h + 1) % 2)
                        emit_avz(h, h % 2)
                ti0 = 4 * a + 2 * half
                for tih in range(2):
                    ti = ti0 + tih
                    xs_sb, xs_buf = xring[ti % 3], xrB[ti % 3]
                    dma("sync", xs_sb[:],
                        d_xs[(ti + 4) * 128:(ti + 5) * 128, :],
                        writes=[xs_buf])
                for op_pair in range(2):
                    for hc in range(NHC):
                        wop = hc % 2
                        dma("scalar", wob[wop][:],
                            d_wo[hc * 128:(hc + 1) * 128,
                                 op_pair * 1024:(op_pair + 1) * 1024],
                            writes=[woB[wop]])
                        def opmm(e, hc=hc, wop=wop):
                            for tih in range(2):
                                for occi in range(2):
                                    inst = e.matmul(
                                        pb[tih * 2 + occi][:],
                                        concat[:, (tih * NHC + hc) * 128:
                                               (tih * NHC + hc) * 128 + 128],
                                        wob[wop][:, occi * 512:
                                                 (occi + 1) * 512],
                                        start=(hc == 0), stop=(hc == NHC - 1))
                            return inst
                        S.step("tensor", opmm,
                               reads=[concatB, woB[wop]],
                               writes=[pbB[0], pbB[1], pbB[2], pbB[3]])
                    for tih in range(2):
                        ti = ti0 + tih
                        xs_sb, xs_buf = xring[ti % 3], xrB[ti % 3]
                        ost, ostB = pool2k[tih], p2kB[tih]
                        for occi in range(2):
                            col = (op_pair * 2 + occi) * 512
                            S.step("vector",
                                   lambda e, tih=tih, occi=occi, col=col,
                                          ost=ost, xs_sb=xs_sb: e.tensor_add(
                                       ost[:, col:col + 512],
                                       pb[tih * 2 + occi][:],
                                       xs_sb[:, col:col + 512]),
                                   reads=[pbB[tih * 2 + occi], xs_buf],
                                   writes=[ostB])
                for tih in range(2):
                    ti = ti0 + tih
                    dma("sync", d_out[ti * 128:(ti + 1) * 128, :],
                        pool2k[tih][:], reads=[p2kB[tih]],
                        writes=[gb("outrows")])



    if "B" in phases:
        S.barrier()
        idxB = gb("b_idx")
        dma("gpsimd", b_idx[:], d_didx.rearrange("(j p) -> p j", p=128),
            writes=[idxB], allow_slow_non_contiguous=True)
        bwB = [gb(f"b_w{i}") for i in range(2)]
        bw8B = [gb(f"b_w8{i}") for i in range(2)]
        bstB = [gb(f"b_st{i}") for i in range(2)]
        bmB = gb("b_m")
        wsgB, wsuB, ws2B = gb("b_wsg"), gb("b_wsu"), gb("b_ws2")
        scrB = {"g": gb("scr_g8"), "u": gb("scr_u8"), "d": gb("scr_d8")}

        for wname, d_src, d_scr, ws_sb, ws_buf in (
                ("g", d_wg, d_g8, b_wsg, wsgB), ("u", d_wu, d_u8, b_wsu, wsuB)):
            for j in range(NJ):
                par = j % 2
                mc = par * 2
                S.step("gpsimd",
                       lambda e, par=par, j=j, d_src=d_src:
                       e.indirect_dma_start(
                           out=b_w[par][:, 0:H], out_offset=None,
                           in_=d_src,
                           in_offset=IndirectOffsetOnAxis(b_idx[:, j:j + 1], 0)),
                       reads=[idxB], writes=[bwB[par]], dma=True)
                S.step("vector",
                       lambda e, par=par, mc=mc: e.reduce_max(
                           b_m[:, mc:mc + 1], b_w[par][:, 0:H], AX.X,
                           apply_absolute_value=True),
                       reads=[bwB[par]], writes=[bmB], narrow=True)
                S.step("vector",
                       lambda e, mc=mc: e.reciprocal(b_m[:, mc + 1:mc + 2],
                                                     b_m[:, mc:mc + 1]),
                       reads=[bmB], writes=[bmB], narrow=True)
                S.step("vector",
                       lambda e, mc=mc: e.tensor_scalar(
                           b_m[:, mc + 1:mc + 2], b_m[:, mc + 1:mc + 2],
                           224.0, None, ALU.mult),
                       reads=[bmB], writes=[bmB], narrow=True)
                S.step("vector",
                       lambda e, j=j, ws_sb=ws_sb, mc=mc: e.tensor_scalar(
                           ws_sb[:, j:j + 1], b_m[:, mc:mc + 1],
                           1.0 / 224.0, None, ALU.mult),
                       reads=[bmB], writes=[ws_buf], narrow=True)
                S.step("vector",
                       lambda e, par=par, mc=mc: e.tensor_scalar(
                           b_w8[par][:, 0:H], b_w[par][:, 0:H],
                           b_m[:, mc + 1:mc + 2], None, ALU.mult),
                       reads=[bwB[par], bmB], writes=[bw8B[par]])
                for grp in range(2):
                    bank = pb[4 + grp]
                    bankB = pbB[4 + grp]
                    def tr8(e, par=par, grp=grp):
                        bk8 = pb[4 + grp][:].bitcast(FP8)
                        for k in range(8):
                            c = grp * 8 + k
                            inst = e.transpose(
                                bk8[:, k * 256:(k + 1) * 256:2],
                                b_w8[par][:, c * 128:(c + 1) * 128],
                                ident8[:])
                        return inst
                    S.step("tensor", tr8,
                           reads=[bw8B[par], gb("ident8")], writes=[bankB])
                    def ev8(e, par=par, grp=grp):
                        src = pb[4 + grp][:].bitcast(FP8).rearrange(
                            "p (k s) -> p k s", k=8)[:, :, 0:256:2]
                        dst = b_stage[par][:, grp * 1024:(grp + 1) * 1024]\
                            .rearrange("p (k e) -> p k e", k=8)
                        return e.tensor_copy(dst, src)
                    S.step("vector", ev8, reads=[bankB], writes=[bstB[par]])
                dma("scalar", d_scr[j], b_stage[par][:, 0:H],
                    reads=[bstB[par]], writes=[scrB[wname]])
        for t16 in range(NHC):
            par = t16 % 2
            mc = par * 2
            dma("sync", b_w[par][:], d_wd[t16 * 128:(t16 + 1) * 128, :],
                writes=[bwB[par]])
            S.step("vector",
                   lambda e, par=par, mc=mc: e.reduce_max(
                       b_m[:, mc:mc + 1], b_w[par][:], AX.X,
                       apply_absolute_value=True),
                   reads=[bwB[par]], writes=[bmB], narrow=True)
            S.step("vector",
                   lambda e, mc=mc: e.reciprocal(b_m[:, mc + 1:mc + 2],
                                                 b_m[:, mc:mc + 1]),
                   reads=[bmB], writes=[bmB], narrow=True)
            S.step("vector",
                   lambda e, mc=mc: e.tensor_scalar(
                       b_m[:, mc + 1:mc + 2], b_m[:, mc + 1:mc + 2],
                       224.0, None, ALU.mult),
                   reads=[bmB], writes=[bmB], narrow=True)
            S.step("vector",
                   lambda e, t16=t16, mc=mc: e.tensor_scalar(
                       b_ws2[:, t16:t16 + 1], b_m[:, mc:mc + 1],
                       1.0 / 224.0, None, ALU.mult),
                   reads=[bmB], writes=[ws2B], narrow=True)
            S.step("vector",
                   lambda e, par=par, mc=mc: e.tensor_scalar(
                       b_w8[par][:], b_w[par][:],
                       b_m[:, mc + 1:mc + 2], None, ALU.mult),
                   reads=[bwB[par], bmB], writes=[bw8B[par]])
            for grp in range(6):
                npc = 8 if grp < 5 else 4
                bank = 4 + grp % 2
                def tr8d(e, par=par, grp=grp, npc=npc, bank=bank):
                    bk8 = pb[bank][:].bitcast(FP8)
                    for k in range(npc):
                        jc = grp * 8 + k
                        inst = e.transpose(
                            bk8[:, k * 256:(k + 1) * 256:2],
                            b_w8[par][:, jc * 128:(jc + 1) * 128],
                            ident8[:])
                    return inst
                S.step("tensor", tr8d,
                       reads=[bw8B[par], gb("ident8")], writes=[pbB[bank]])
                def ev8d(e, par=par, grp=grp, npc=npc, bank=bank):
                    src = pb[bank][:].bitcast(FP8).rearrange(
                        "p (k s) -> p k s", k=8)[:, 0:npc, 0:256:2]
                    dst = b_stage[par][:, grp * 1024:grp * 1024 + npc * 128]\
                        .rearrange("p (k e) -> p k e", k=npc)
                    return e.tensor_copy(dst, src)
                S.step("vector", ev8d, reads=[pbB[bank]], writes=[bstB[par]])
            dma("scalar", d_d8[t16], b_stage[par][:],
                reads=[bstB[par]], writes=[scrB["d"]])

    if "C" in phases:
        S.barrier()
        cxB = [gb(f"c_x{i}") for i in range(2)]
        cxnB = gb("c_xn")
        cstB = gb("c_stats")
        cq8B, cq8TB = gb("c_qx8"), gb("c_qx8T")
        cfxB, cfxbB = gb("c_fx"), gb("c_fxb")
        cwgB = [gb(f"c_wg8{i}") for i in range(2)]
        cwuB = [gb(f"c_wu8{i}") for i in range(2)]
        cwdB = [gb(f"c_wd8{i}") for i in range(2)]
        ctB, crmB, cqtB = gb("c_t"), gb("c_rmax"), gb("c_qt8")
        crmnB = gb("c_rmin")
        cgscB, cuscB, cgsiB = gb("c_gsc"), gb("c_usc"), gb("c_gsi")
        cqsbB, cftbB = gb("c_qsb"), gb("c_ftb")
        cy2B = [gb(f"c_y2{i}") for i in range(2)]
        costB = [gb(f"c_ost{i}") for i in range(2)]
        crowB, cmtB = gb("c_row"), gb("c_mrow")
        scrB2 = {"g": gb("scr_g8"), "u": gb("scr_u8"), "d": gb("scr_d8")}
        wsgB, wsuB, ws2B = gb("b_wsg"), gb("b_wsu"), gb("b_ws2")
        UTIL = pb[4]
        UTILB = pbB[4]
        UTIL2 = pb[5]
        UTIL2B = pbB[5]

        for cc in range(NCH):
            for bp in range(2):
                tb = cc * 2 + bp
                S_x, S_xB = c_x[bp], cxB[bp]
                dma("sync", S_x[:], d_xs[(tb + 4) * 128:(tb + 5) * 128, :],
                    writes=[S_xB])
                emit_ln_g(S_x, S_xB, c_xn, cxnB, c_ost[0], costB[0],
                          c_stats, cstB)
                S.step("vector",
                       lambda e: e.reduce_max(c_stats[:, 5:6], c_xn[:], AX.X,
                                              apply_absolute_value=True),
                       reads=[cxnB], writes=[cstB], narrow=True)
                S.step("vector",
                       lambda e: e.reciprocal(c_stats[:, 6:7], c_stats[:, 5:6]),
                       reads=[cstB], writes=[cstB], narrow=True)
                S.step("vector",
                       lambda e: e.tensor_scalar(c_stats[:, 6:7],
                                                 c_stats[:, 6:7], 224.0, None,
                                                 ALU.mult),
                       reads=[cstB], writes=[cstB], narrow=True)
                S.step("vector",
                       lambda e, bp=bp: e.tensor_scalar(
                           c_fx[:, bp:bp + 1], c_stats[:, 5:6],
                           1.0 / 224.0, None, ALU.mult),
                       reads=[cstB], writes=[cfxB], narrow=True)
                S.step("vector",
                       lambda e: e.tensor_scalar(c_qx8[:], c_xn[:],
                                                 c_stats[:, 6:7], None,
                                                 ALU.mult),
                       reads=[cxnB, cstB], writes=[cq8B])
                if cc == TAP_CC and bp == 0:
                    tap("c_xn0", c_xn[:], cxnB)
                    tap("c_qx80", c_qx8[:], cq8B)
                    tap("c_stats0", c_stats[:], cstB)
                for grp in range(2):
                    bank = (UTIL, UTIL2)[grp]
                    bankB = (UTILB, UTIL2B)[grp]
                    def trq(e, grp=grp, bank=bank):
                        bk8 = bank[:].bitcast(FP8)
                        for k in range(8):
                            hc = grp * 8 + k
                            inst = e.transpose(
                                bk8[:, k * 256:(k + 1) * 256:2],
                                c_qx8[:, hc * 128:(hc + 1) * 128], ident8[:])
                        return inst
                    S.step("tensor", trq, reads=[cq8B, gb("ident8")],
                           writes=[bankB])
                    def evq(e, grp=grp, bp=bp, bank=bank):
                        src = bank[:].bitcast(FP8).rearrange(
                            "p (k s) -> p k s", k=8)[:, :, 0:256:2]
                        dst = bass.AP(c_qx8T, (grp * 8) * CC + bp * 128,
                                      [[NHC * CC, 128], [CC, 8], [1, 128]])
                        return e.tensor_copy(dst, src)
                    S.step("vector", evq, reads=[bankB], writes=[cq8TB])
                S.step("tensor",
                       lambda e, bp=bp: e.matmul(
                           pb[0][0:1, bp * 128:(bp + 1) * 128],
                           c_fx[:, bp:bp + 1], ident[:],
                           start=(bp == 0), stop=(bp == 1)),
                       reads=[cfxB, gb("ident")], writes=[pbB[0]])
            S.step("vector",
                   lambda e: e.tensor_copy(c_row[0:1, 0:CC], pb[0][0:1, 0:CC]),
                   reads=[pbB[0]], writes=[crowB], narrow=True)
            S.step("tensor",
                   lambda e: e.matmul(UTIL2[:, 0:CC], ones[0:1, :],
                                      c_row[0:1, 0:CC], start=True, stop=True),
                   reads=[crowB, gb("ones")], writes=[UTIL2B])
            S.step("vector",
                   lambda e: e.tensor_copy(c_fxb[:], UTIL2[:, 0:CC]),
                   reads=[UTIL2B], writes=[cfxbB])
            if cc == TAP_CC:
                tap("c_qx8T0", c_qx8T[:], cq8TB)
                tap("c_fxb0", c_fxb[:], cfxbB)
            for j in range(NJ):
                par = j % 2
                pg, pgB = pb[2 * par], pbB[2 * par]
                pu, puB = pb[2 * par + 1], pbB[2 * par + 1]
                dma("sync", c_wg8[par][:], d_g8[j], writes=[cwgB[par]])
                dma("sync", c_wu8[par][:], d_u8[j], writes=[cwuB[par]])
                DR = mybir.MatmulPerfMode.DoubleRow
                def gmm_plain(e, par=par, pg=pg):
                    for hc in range(NHC):
                        inst = e.matmul(pg[:, 0:CC],
                                        c_wg8[par][:, hc * 128:(hc + 1) * 128],
                                        c_qx8T[:, hc * CC:(hc + 1) * CC],
                                        start=(hc == 0), stop=(hc == NHC - 1))
                    return inst
                def umm_plain(e, par=par, pu=pu):
                    for hc in range(NHC):
                        inst = e.matmul(pu[:, 0:CC],
                                        c_wu8[par][:, hc * 128:(hc + 1) * 128],
                                        c_qx8T[:, hc * CC:(hc + 1) * CC],
                                        start=(hc == 0), stop=(hc == NHC - 1))
                    return inst
                def gmm(e, par=par, pg=pg):
                    for h2 in range(NHC // 2):
                        inst = e.matmul(
                            pg[:, 0:CC],
                            c_wg8[par][:, h2 * 256:(h2 + 1) * 256]
                            .rearrange("p (i m) -> p i m", i=2),
                            c_qx8T[:, h2 * 2 * CC:(h2 + 1) * 2 * CC]
                            .rearrange("p (i n) -> p i n", i=2),
                            start=(h2 == 0), stop=(h2 == NHC // 2 - 1),
                            perf_mode=DR)
                    return inst
                S.step("tensor", gmm if USE_DR_GU else gmm_plain,
                       reads=[cwgB[par], cq8TB], writes=[pgB])
                def umm(e, par=par, pu=pu):
                    for h2 in range(NHC // 2):
                        inst = e.matmul(
                            pu[:, 0:CC],
                            c_wu8[par][:, h2 * 256:(h2 + 1) * 256]
                            .rearrange("p (i m) -> p i m", i=2),
                            c_qx8T[:, h2 * 2 * CC:(h2 + 1) * 2 * CC]
                            .rearrange("p (i n) -> p i n", i=2),
                            start=(h2 == 0), stop=(h2 == NHC // 2 - 1),
                            perf_mode=DR)
                    return inst
                S.step("tensor", umm if USE_DR_GU else umm_plain,
                       reads=[cwuB[par], cq8TB], writes=[puB])
                S.step("vector",
                       lambda e, j=j, pg=pg: e.scalar_tensor_tensor(
                           c_gsc[:], pg[:, 0:CC], b_wsg[:, j:j + 1], c_fxb[:],
                           ALU.mult, ALU.mult),
                       reads=[pgB, wsgB, cfxbB], writes=[cgscB])
                S.step("scalar",
                       lambda e: e.activation(c_gsi[:], c_gsc[:], AF.Silu),
                       reads=[cgscB], writes=[cgsiB])
                S.step("vector",
                       lambda e, j=j, pu=pu: e.scalar_tensor_tensor(
                           c_usc[:], pu[:, 0:CC], b_wsu[:, j:j + 1], c_fxb[:],
                           ALU.mult, ALU.mult),
                       reads=[puB, wsuB, cfxbB], writes=[cuscB])
                if cc == TAP_CC and j == TAP_J:
                    tap("c_gsc0", c_gsc[:], cgscB)
                    tap("c_usc0", c_usc[:], cuscB)
                    tap("c_wg80", c_wg8[j % 2][:], cwgB[j % 2])
                S.step("gpsimd",
                       lambda e, j=j: e.tensor_mul(
                           c_t[:, j * CC:(j + 1) * CC], c_gsi[:], c_usc[:]),
                       reads=[cgsiB, cuscB], writes=[ctB])
                if j == 0:
                    S.step("vector",
                           lambda e: e.tensor_copy(c_rmax[:], c_t[:, 0:CC]),
                           reads=[ctB], writes=[crmB])
                    S.step("vector",
                           lambda e: e.tensor_copy(c_rmin[:], c_t[:, 0:CC]),
                           reads=[ctB], writes=[crmnB])
                else:
                    S.step("vector",
                           lambda e, j=j: e.tensor_max(
                               c_rmax[:], c_rmax[:],
                               c_t[:, j * CC:(j + 1) * CC]),
                           reads=[ctB, crmB], writes=[crmB])
                    S.step("vector",
                           lambda e, j=j: e.tensor_tensor(
                               c_rmin[:], c_rmin[:],
                               c_t[:, j * CC:(j + 1) * CC], ALU.min),
                           reads=[ctB, crmnB], writes=[crmnB])
            S.step("vector",
                   lambda e: e.tensor_scalar(c_rmin[:], c_rmin[:], -1.0,
                                             None, ALU.mult),
                   reads=[crmnB], writes=[crmnB])
            S.step("vector",
                   lambda e: e.tensor_max(c_rmax[:], c_rmax[:], c_rmin[:]),
                   reads=[crmnB, crmB], writes=[crmB])
            if cc == TAP_CC:
                tap("c_t0", c_t[:], ctB)
                tap("c_rmax0", c_rmax[:], crmB)
                tap("b_wsg0", b_wsg[:], wsgB)
                tap("b_ws20", b_ws2[:], ws2B)
            for bp in range(2):
                S.step("tensor",
                       lambda e, bp=bp: e.transpose(
                           UTIL[:, bp * 128:(bp + 1) * 128],
                           c_rmax[:, bp * 128:(bp + 1) * 128], ident[:]),
                       reads=[crmB, gb("ident")], writes=[UTILB])
                S.step("vector",
                       lambda e, bp=bp: e.reduce_max(
                           c_mrow[:, bp:bp + 1],
                           UTIL[:, bp * 128:(bp + 1) * 128], AX.X),
                       reads=[UTILB], writes=[cmtB], narrow=True)
                S.step("tensor",
                       lambda e, bp=bp: e.matmul(
                           UTIL2[0:1, bp * 128:(bp + 1) * 128],
                           c_mrow[:, bp:bp + 1], ident[:],
                           start=(bp == 0), stop=(bp == 1)),
                       reads=[cmtB, gb("ident")], writes=[UTIL2B])
            S.step("vector",
                   lambda e: e.tensor_copy(c_row[0:1, 256:256 + CC],
                                           UTIL2[0:1, 0:CC]),
                   reads=[UTIL2B], writes=[crowB], narrow=True)
            S.step("vector",
                   lambda e: e.reciprocal(c_row[0:1, 0:CC],
                                          c_row[0:1, 256:256 + CC]),
                   reads=[crowB], writes=[crowB], narrow=True)
            S.step("vector",
                   lambda e: e.tensor_scalar(c_row[0:1, 0:CC],
                                             c_row[0:1, 0:CC], 224.0, None,
                                             ALU.mult),
                   reads=[crowB], writes=[crowB], narrow=True)
            S.step("vector",
                   lambda e: e.tensor_scalar(c_row[0:1, 256:256 + CC],
                                             c_row[0:1, 256:256 + CC],
                                             1.0 / 224.0, None, ALU.mult),
                   reads=[crowB], writes=[crowB], narrow=True)
            S.step("tensor",
                   lambda e: e.matmul(UTIL[:, 0:CC], ones[0:1, :],
                                      c_row[0:1, 0:CC], start=True, stop=True),
                   reads=[crowB, gb("ones")], writes=[UTILB])
            S.step("vector", lambda e: e.tensor_copy(c_qsb[:], UTIL[:, 0:CC]),
                   reads=[UTILB], writes=[cqsbB])
            S.step("tensor",
                   lambda e: e.matmul(UTIL[:, 0:CC], ones[0:1, :],
                                      c_row[0:1, 256:256 + CC],
                                      start=True, stop=True),
                   reads=[crowB, gb("ones")], writes=[UTILB])
            S.step("vector", lambda e: e.tensor_copy(c_ftb[:], UTIL[:, 0:CC]),
                   reads=[UTILB], writes=[cftbB])
            if cc == TAP_CC:
                tap("c_qsb0", c_qsb[:], cqsbB)
                tap("c_ftb0", c_ftb[:], cftbB)
            for j in range(NJ):
                S.step("vector",
                       lambda e, j=j: e.tensor_tensor(
                           c_qt8[:, j * CC:(j + 1) * CC],
                           c_t[:, j * CC:(j + 1) * CC], c_qsb[:], ALU.mult),
                       reads=[ctB, cqsbB], writes=[cqtB])
            if cc == TAP_CC:
                tap("c_qt80", c_qt8[:], cqtB)
            for hc in range(NHC):
                par = hc % 2
                pd, pdB = pb[2 + par], pbB[2 + par]
                dma("sync", c_wd8[par][:], d_d8[hc], writes=[cwdB[par]])
                DR = mybir.MatmulPerfMode.DoubleRow
                def dmm_plain(e, par=par, pd=pd):
                    for jc in range(NJ):
                        inst = e.matmul(pd[:, 0:CC],
                                        c_wd8[par][:, jc * 128:(jc + 1) * 128],
                                        c_qt8[:, jc * CC:(jc + 1) * CC],
                                        start=(jc == 0), stop=(jc == NJ - 1))
                    return inst
                def dmm(e, par=par, pd=pd):
                    for j2 in range(NJ // 2):
                        inst = e.matmul(
                            pd[:, 0:CC],
                            c_wd8[par][:, j2 * 256:(j2 + 1) * 256]
                            .rearrange("p (i m) -> p i m", i=2),
                            c_qt8[:, j2 * 2 * CC:(j2 + 1) * 2 * CC]
                            .rearrange("p (i n) -> p i n", i=2),
                            start=(j2 == 0), stop=(j2 == NJ // 2 - 1),
                            perf_mode=DR)
                    return inst
                S.step("tensor", dmm if USE_DR_DN else dmm_plain,
                       reads=[cwdB[par], cqtB], writes=[pdB])
                S.step("vector",
                       lambda e, hc=hc, par=par, pd=pd: e.scalar_tensor_tensor(
                           c_y2[par][:, 0:CC], pd[:, 0:CC],
                           b_ws2[:, hc:hc + 1], c_ftb[:], ALU.mult, ALU.mult),
                       reads=[pdB, ws2B, cftbB], writes=[cy2B[par]])
                for bp in range(2):
                    S.step("tensor",
                           lambda e, hc=hc, par=par, bp=bp: e.transpose(
                               UTIL[:, bp * 128:(bp + 1) * 128],
                               c_y2[par][:, bp * 128:(bp + 1) * 128],
                               ident[:]),
                           reads=[cy2B[par], gb("ident")], writes=[UTILB])
                    S.step("vector",
                           lambda e, hc=hc, bp=bp: e.tensor_copy(
                               c_ost[bp][:, hc * 128:(hc + 1) * 128],
                               UTIL[:, bp * 128:(bp + 1) * 128]),
                           reads=[UTILB], writes=[costB[bp]])
            for bp in range(2):
                tb = cc * 2 + bp
                dma("gpsimd", d_out[tb * 128:(tb + 1) * 128, :],
                    c_ost[bp][:], accum_op=ALU.add,
                    reads=[costB[bp]], writes=[gb("outrows")])

    S.barrier()
    final_waits = [(n, c) for n, c in S.cnt.items() if n.startswith("q_")]
    S.step("gpsimd", lambda e: e.nop(), extra_waits=final_waits)

    sems = {}
    from contextlib import ExitStack
    with ExitStack() as stack:
        for name in S.cnt:
            sems[name] = stack.enter_context(nc.semaphore(name))
        S.finalize(nc, sems)
    return nc


def assemble(results):
    out = np.empty((BATCH, T, H), dtype=np.float32)
    for c in range(NCORE):
        b, hh = c // 2, c % 2
        out[b, hh * TOK:(hh + 1) * TOK] = results[c]["out"]
    return out


def kernel(**inputs):
    in_maps = stage_inputs(**inputs)
    nc = build_program()
    res = run_bass_kernel_spmd(nc, in_maps, core_ids=list(range(NCORE)))
    return assemble(res.results)
